# version 18
# baseline (speedup 1.0000x reference)
import contextlib
import numpy as np
import ml_dtypes
import concourse.bass as bass
import concourse.mybir as mybir
from concourse.bass_utils import run_bass_kernel_spmd

F32 = mybir.dt.float32
BF16 = mybir.dt.bfloat16
I32 = mybir.dt.int32
RMAGIC = 12582912.0
AF = mybir.ActivationFunctionType
ALU = mybir.AluOpType
AX = mybir.AxisListType

B, L, D = 4, 8192, 1024
INNER = 4096
HALF = 4096
TC = 512
NSC = HALF // TC
TCH = 128
NT = TC // TCH
KC = D // 128
ICN = INNER // 128
NREG, NMEM = 8, 16
DECAY, SHARP = 0.995, 5.0
DEC128 = float(DECAY ** 128)
EPS = 1e-5
N_CORES = 8

NLD, NST, NLS = 8, 4, 4
LD_SEMS = tuple(f"sLD{i}" for i in range(NLD))
ST_SEMS = tuple(f"sST{i}" for i in range(NST))
LS_SEMS = tuple(f"sLS{i}" for i in range(NLS))
SEMS = LD_SEMS + ST_SEMS + LS_SEMS + ("sT", "sV", "sS", "sCCs")



def MMSG(t, out, lhsT, rhs, start=True, stop=True):
    return t.matmul(out, lhsT, rhs, start=start, stop=stop, skip_group_check=True)

class Wk:

    def __init__(self, E, eng, sems):
        self.E, self.eng, self.sems = E, eng, sems
        self.c = {k: 0 for k in SEMS}
        self.notes = {}
        self.psum_cons = {}
        self.bank_started = {}
        self.wb_lastuse = [0, 0, 0, 0]
        self.ld_rot = 0
        self.st_rot = 0
        self.ls_rot = 0

    def do(self, e, fn):
        return fn(self.eng) if self.E == e else None

    def inc(self, sem, instr, n=1):
        self.c[sem] += n
        if instr is not None:
            instr.then_inc(self.sems[sem], n)
        return self.c[sem]

    def wait(self, e, sem, val):
        if self.E == e and val > 0:
            self.eng.wait_ge(self.sems[sem], val)

    def wait_t(self, e, tok):
        self.wait(e, tok[0], tok[1])

    def bar(self, engines=("t", "v", "s")):
        if self.E == "v":
            self.eng.drain()
        if self.E == "s":
            self.eng.drain()
        snap = {"t": self.c["sT"], "v": self.c["sV"], "s": self.c["sS"]}
        sem_of = {"t": "sT", "v": "sV", "s": "sS"}
        for e in engines:
            if self.E == e:
                for o in engines:
                    if o != e and snap[o] > 0:
                        self.eng.wait_ge(self.sems[sem_of[o]], snap[o])

    def bank_open(self, bank):
        if bank in self.psum_cons:
            sem, cnt = self.psum_cons[bank]
            self.wait("t", sem, cnt)
        self.bank_started[bank] = False

    def bank_start(self, bank):
        if not self.bank_started.get(bank, False):
            self.bank_started[bank] = True
            return True
        return False

    def bank_consumed(self, bank, sem="sV"):
        self.psum_cons[bank] = (sem, self.c[sem])


def build_program(stage="full", nsc=NSC, n_replicas=8):
    nc = bass.Bass("TRN2", target_bir_lowering=False, debug=False)
    inp = {}

    def din(name, shape, dt=F32):
        inp[name] = nc.dram_tensor(name, shape, dt, kind="ExternalInput").ap()
        return inp[name]

    xT_own = din("xT_own", [D, HALF])
    wgT = din("wgT", [D, INNER])
    wuT = din("wuT", [D, INNER])
    wdT = din("wdT", [INNER, D])
    wvrT = din("wvrT", [D, D])
    wqrT = din("wqrT", [D, D])
    wvmT = din("wvmT", [D, D])
    wqmT = din("wqmT", [D, D])
    combT = din("combT", [3 * D, D])
    wzT = din("wzT", [D, 32])
    zbrow = din("zbrow", [1, 32])
    bvr_pp = din("bvr_pp", [128, KC])
    bqr_pp = din("bqr_pp", [128, KC])
    bvm_pp = din("bvm_pp", [128, KC])
    bqm_pp = din("bqm_pp", [128, KC])
    combb_pp = din("combb_pp", [128, KC])
    lng_pp = din("lng_pp", [128, KC])
    lnb_pp = din("lnb_pp", [128, KC])
    bvr_row = din("bvr_row", [1, D])
    bvm_row = din("bvm_row", [1, D])
    maskUL = din("maskUL", [128, 128])
    maskDec = din("maskDec", [128, 128])
    maskDecE = din("maskDecE", [128, 128])
    lam_in = din("lam_in", [128, 1])
    ddv_in = din("ddv_in", [128, 1])
    ident_in = din("ident_in", [128, 128], BF16)
    identf_in = din("identf_in", [128, 128])
    ones_in = din("ones_in", [128, 1])
    onesbf_in = din("onesbf_in", [128, 1], BF16)
    bc1_in = din("bc1_in", [1, 128])
    sel_in = din("sel_in", [128, 8])

    yT = nc.dram_tensor("yT", [D, HALF], F32, kind="ExternalOutput").ap()

    sx1 = nc.dram_tensor("sx1", [D, HALF], F32, kind="Internal").ap()
    sy2 = nc.dram_tensor("sy2", [D, HALF], F32, kind="Internal").ap()
    sy2b = nc.dram_tensor("sy2b", [D, HALF], BF16, kind="Internal").ap()
    srv = nc.dram_tensor("srv", [D, HALF], BF16, kind="Internal").ap()
    srq = nc.dram_tensor("srq", [D, HALF], BF16, kind="Internal").ap()
    smv = nc.dram_tensor("smv", [D, HALF], BF16, kind="Internal").ap()
    smq = nc.dram_tensor("smq", [D, HALF], BF16, kind="Internal").ap()
    sabf = nc.dram_tensor("sabf", [128, NSC * NT * 32], BF16, kind="Internal").ap()
    sloc = nc.dram_tensor("sloc", [128, NSC * NT * 192], F32, kind="Internal").ap()
    ccin = nc.dram_tensor("ccin", [1, 128 * 192], F32, kind="Internal").ap()
    ccout = nc.dram_tensor("ccout", [n_replicas, 128 * 192], F32, kind="Internal").ap()
    wgq = nc.dram_tensor("wgq", [D, INNER], BF16, kind="Internal").ap()
    wuq = nc.dram_tensor("wuq", [D, INNER], BF16, kind="Internal").ap()
    wdq = nc.dram_tensor("wdq", [INNER, D], BF16, kind="Internal").ap()
    wvrq = nc.dram_tensor("wvrq", [D, D], BF16, kind="Internal").ap()
    wqrq = nc.dram_tensor("wqrq", [D, D], BF16, kind="Internal").ap()
    wvmq = nc.dram_tensor("wvmq", [D, D], BF16, kind="Internal").ap()
    wqmq = nc.dram_tensor("wqmq", [D, D], BF16, kind="Internal").ap()
    combq = nc.dram_tensor("combq", [3 * D, D], BF16, kind="Internal").ap()

    es = contextlib.ExitStack()
    with es:
        sb = lambda n, sh, dt=F32: es.enter_context(nc.sbuf_tensor(n, sh, dt))
        psal = lambda n, sh, dt=F32: es.enter_context(nc.psum_tensor(n, sh, dt))

        R = sb("R", [128, KC * TC])
        ARN = sb("ARN", [128, KC * TC])
        MIDF = sb("MIDF", [128, 24 * 512])
        Y2B = sb("Y2B", [128, KC * TC], BF16)
        K1 = sb("K1", [128, KC * TC], BF16)
        RVB = sb("RVB", [128, KC * TC], BF16)
        RQB = sb("RQB", [128, KC * TC], BF16)
        MVB = sb("MVB", [128, KC * TC], BF16)
        MQB = sb("MQB", [128, KC * TC], BF16)
        VrTM = sb("VrTM", [128, D], BF16)
        VmTM = sb("VmTM", [128, D], BF16)
        QSCW = sb("QSCW", [128, 4 * 512])
        WB = [sb(f"WB{i}", [128, 4096], BF16) for i in range(3)]
        WZ = sb("WZ", [128, KC * 32], BF16)
        ZBB = sb("ZBB", [128, 32])
        GT8 = sb("GT8", [128, 2 * NT])
        BVr = sb("BVr", [128, KC])
        BQr = sb("BQr", [128, KC])
        BVm = sb("BVm", [128, KC])
        BQm = sb("BQm", [128, KC])
        CBB = sb("CBB", [128, KC])
        LNG = sb("LNG", [128, KC])
        LNB = sb("LNB", [128, KC])
        MKUL = sb("MKUL", [128, 128])
        MKDC = sb("MKDC", [128, 128])
        MKDE = sb("MKDE", [128, 128])
        LAM = sb("LAM", [128, 1])
        DDV = sb("DDV", [128, 1])
        IDN = sb("IDN", [128, 128], BF16)
        IDF = sb("IDF", [128, 128])
        ONE = sb("ONE", [128, 1])
        ONEBF = sb("ONEBF", [128, 1], BF16)
        BC1 = sb("BC1", [1, 128])
        CON = sb("CON", [128, 8])
        SRF = sb("SRF", [128, KC * NREG])
        SMF = sb("SMF", [128, KC * NMEM])
        SRFB = sb("SRFB", [128, KC * NREG], BF16)
        SMFB = sb("SMFB", [128, KC * NMEM], BF16)
        SRNB = sb("SRNB", [NREG, D], BF16)
        SMNB = sb("SMNB", [NMEM, D], BF16)
        ABT = sb("ABT", [128, NT * 32])
        ABF = sb("ABF", [128, NT * 32], BF16)
        BDCB = sb("BDCB", [128, NMEM], BF16)
        P1B = sb("P1B", [128, 256], BF16)
        PR = sb("PR", [128, NREG])
        PM = sb("PM", [128, NMEM])
        PRB = sb("PRB", [128, NREG], BF16)
        PMB = sb("PMB", [128, NMEM], BF16)
        SCM = sb("SCM", [128, NMEM])
        SMX = sb("SMX", [128, 24])
        TRS = sb("TRS", [16, 512], BF16)
        EMB = sb("EMB", [128, 256], BF16)
        ROW = [sb(f"ROW{i}", [1, TC]) for i in range(5)]
        ROW.append(ROW[0])
        RD = sb("RD", [1, D])
        RDB = sb("RDB", [1, D], BF16)
        MW = sb("MW", [1, 8])
        SAB = sb("SAB", [128, 1])
        BRD = sb("BRD", [128, TC])
        BRDG = sb("BRDG", [128, TC])
        SINI = sb("SINI", [128, 192])
        SEL = sb("SEL", [128, 8])
        SQ2 = sb("SQ2", [128, TC])
        SQ2B = sb("SQ2B", [128, TC])

        MX1 = sb("MX1", [128, TC])
        SMAG = MX1
        MXT = sb("MXT", [128, 2 * NT])

        Y2F = MIDF
        GATH = QSCW
        SLOC = QSCW[:, 1280:1280 + NT * 192]
        SNM = QSCW
        GBF = VrTM
        GFB = VmTM
        RRD = RVB
        MRD = MVB

        pb = [psal(f"pb{i}", [128, 512]) for i in range(6)]
        pstat = psal("pstat", [128, 512])
        ptr = psal("ptr", [128, 1024], BF16)

        sems = {k: es.enter_context(nc.semaphore(k)) for k in SEMS}
        block = es.enter_context(nc.Block())

        def schedule(w: Wk):
            T, V, S, Y, Gq = "t", "v", "s", "y", "g"

            def load(dst_ap, src_ap):
                k = LD_SEMS[w.ld_rot % NLD]
                w.ld_rot += 1
                w.wait(Y, k, w.c[k])
                i = w.do(Y, lambda y: y.dma_start(out=dst_ap, in_=src_ap))
                return (k, w.inc(k, i, 16))

            def loadS(dst_ap, src_ap):
                k = LS_SEMS[w.ls_rot % NLS]
                w.ls_rot += 1
                w.wait(S, k, w.c[k])
                i = w.do(S, lambda s_: s_.dma_start(out=dst_ap, in_=src_ap))
                return (k, w.inc(k, i, 16))

            def store(dst_ap, src_ap, after=None):
                if after is None:
                    after = ("sV", w.c["sV"])
                k = ST_SEMS[w.st_rot % NST]
                w.st_rot += 1
                w.wait(Gq, k, w.c[k])
                w.wait_t(Gq, after)
                i = w.do(Gq, lambda g: g.dma_start(out=dst_ap, in_=src_ap))
                return (k, w.inc(k, i, 16))

            def wait_all_st(e):
                for k in ST_SEMS:
                    w.wait(e, k, w.c[k])

            def fmview(dram, r0, nrow, c0, ncol):
                return dram[r0:r0 + nrow, c0:c0 + ncol].rearrange(
                    "(c p) n -> p c n", p=128)

            def sb3(sb_ap, ncol):
                return sb_ap.rearrange("p (c n) -> p c n", n=ncol)

            def vop(fn):
                return w.inc("sV", w.do(V, fn))

            def vopd(fn):
                w.do(V, lambda v: v.drain())
                return w.inc("sV", w.do(V, fn))

            def sop(fn):
                return w.inc("sS", w.do(S, fn))

            def sopd(fn):
                w.do(S, lambda s: s.drain())
                return w.inc("sS", w.do(S, fn))

            def top(fn):
                return w.inc("sT", w.do(T, fn))

            load(MKUL[:], maskUL)
            load(MKDC[:], maskDec)
            load(MKDE[:], maskDecE)
            load(LAM[:], lam_in)
            load(DDV[:], ddv_in)
            load(IDN[:], ident_in)
            load(IDF[:], identf_in)
            load(ONE[:], ones_in)
            load(ONEBF[:], onesbf_in)
            load(BC1[:], bc1_in)
            load(BVr[:], bvr_pp)
            load(BQr[:], bqr_pp)
            load(BVm[:], bvm_pp)
            load(BQm[:], bqm_pp)
            load(CBB[:], combb_pp)
            load(LNG[:], lng_pp)
            load(LNB[:], lnb_pp)
            for k in LD_SEMS:
                w.wait(V, k, w.c[k])
            vop(lambda v: v.memset(CON[:, 0:1], 0.03125))
            vop(lambda v: v.memset(CON[:, 1:2], -SHARP))
            vop(lambda v: v.memset(CON[:, 2:3], -0.5))
            vop(lambda v: v.memset(CON[:, 3:4], 1.0))
            vop(lambda v: v.memset(CON[:, 4:5], RMAGIC))
            vop(lambda v: v.memset(CON[:, 5:6], -RMAGIC))
            vop(lambda v: v.memset(ABF[:], 0.0))
            vop(lambda v: v.memset(SRF[:], 0.0))
            vop(lambda v: v.memset(SMF[:], 0.0))

            stg = [MIDF[:, i * 2048:(i + 1) * 2048] for i in range(4)]
            stgB = [MIDF[:, 8192 + i * 2048: 8192 + (i + 1) * 2048] for i in range(2)]
            stgB.append(QSCW[:, 0:2048])

            def prep_ffn(matT, scratch_q, nrow, ncol, m_slot):
                colw = min(2048, ncol)
                nch = (nrow // 128) * (ncol // colw)
                ncolc = ncol // colw
                w.wait(S, "sT", w.c["sT"])
                w.wait(Y, "sS", w.c["sS"])
                w.bank_open("pstat")
                for ci in range(nch):
                    r = (ci // ncolc) * 128
                    cc = (ci % ncolc) * colw
                    slot = stg[ci % 4]
                    if ci >= 4:
                        w.wait(Y, "sS", w.notes[("pA", ci - 4)])
                    lv = load(slot[:, 0:colw], matT[r:r + 128, cc:cc + colw])
                    w.wait_t(S, lv)
                    ab = ARN[:, (ci % 2) * 2048:(ci % 2) * 2048 + colw]
                    if ci >= 2:
                        w.wait(S, "sT", w.notes[("pAt", ci - 2)])
                    w.notes[("pA", ci)] = sop(
                        lambda s_, slot=slot, ab=ab: s_.activation(ab, slot[:, 0:colw], AF.Abs))
                    w.wait(T, "sS", w.c["sS"])
                    for qn in range(colw // 512):
                        top(lambda t, ab=ab, qn=qn, ci=ci: MMSG(
                            t, pstat[0:1, 0:512], ONE[:], ab[:, qn * 512:(qn + 1) * 512],
                            start=w.bank_start("pstat"),
                            stop=(ci == nch - 1 and qn == colw // 512 - 1)))
                    w.notes[("pAt", ci)] = w.c["sT"]
                w.wait(V, "sT", w.c["sT"])
                vopd(lambda v: v.tensor_reduce(ROW[0][0:1, 0:1], pstat[0:1, 0:512],
                                               AX.X, ALU.add))
                w.bank_consumed("pstat")
                vopd(lambda v: v.tensor_scalar(ROW[0][0:1, 0:1], ROW[0][0:1, 0:1],
                                              1.0 / (nrow * ncol), 1e-5, ALU.mult, ALU.max))
                vopd(lambda v, m=m_slot: v.tensor_copy(MW[0:1, m:m + 1], ROW[0][0:1, 0:1]))
                vop(lambda v: v.reciprocal(ROW[0][0:1, 1:2], ROW[0][0:1, 0:1]))
                w.wait(T, "sV", w.c["sV"])
                w.bank_open("pb4")
                top(lambda t: MMSG(t, pb[4][:, 0:1], BC1[:], ROW[0][0:1, 1:2],
                                       start=w.bank_start("pb4"), stop=True))
                w.wait(V, "sT", w.c["sT"])
                vop(lambda v: v.tensor_copy(SQ2[:, 1:2], pb[4][:, 0:1]))
                w.bank_consumed("pb4")
                w.wait(S, "sV", w.c["sV"])
                wait_all_st(V)
                for ci in range(nch):
                    r = (ci // ncolc) * 128
                    cc = (ci % ncolc) * colw
                    slot = stgB[ci % 3]
                    if ci >= 3:
                        w.wait(S, "sV", w.notes[("pBm", ci - 3)])
                    lvB = loadS(slot[:, 0:colw], matT[r:r + 128, cc:cc + colw])
                    par = (ci % 2) * 2048
                    w.wait_t(V, lvB)
                    w.notes[("pBm", ci)] = vop(lambda v, slot=slot, par=par: v.tensor_scalar(
                        R[:, par:par + colw], slot[:, 0:colw], SQ2[:, 1:2], RMAGIC,
                        ALU.mult, ALU.add))
                    vopd(lambda v, par=par: v.tensor_scalar(
                        R[:, par:par + colw], R[:, par:par + colw],
                        -RMAGIC, -1.0, ALU.add, ALU.max))
                    if ci >= 2:
                        w.wait_t(V, w.notes[("pB", ci - 2)])
                    last = vopd(lambda v, par=par: v.tensor_scalar(
                        WB[2][:, par:par + colw], R[:, par:par + colw],
                        1.0, None, ALU.min))
                    w.notes[("pB", ci)] = store(
                        scratch_q[r:r + 128, cc:cc + colw], WB[2][:, par:par + colw],
                        after=("sV", last))

            prep_ffn(wgT, wgq, D, INNER, 0)
            prep_ffn(wuT, wuq, D, INNER, 1)
            prep_ffn(wdT, wdq, INNER, D, 2)
            vopd(lambda v: v.tensor_tensor(MW[0:1, 3:4], MW[0:1, 1:2], MW[0:1, 2:3], ALU.mult))
            vopd(lambda v: v.tensor_scalar(MW[0:1, 3:4], MW[0:1, 3:4], 0.5 / (127.0 * 127.0), None, ALU.mult))
            vopd(lambda v: v.tensor_scalar(MW[0:1, 4:5], MW[0:1, 0:1], 1.0 / 127.0, None, ALU.mult))

            def conv_bf16(matT, scratch_q, nrow):
                w.wait(Y, "sV", w.c["sV"])
                w.wait(Y, "sS", w.c["sS"])
                wait_all_st(S)
                for ci in range(nrow // 128):
                    slot = stg[ci % 4]
                    q = ci % 4
                    if ci >= 4:
                        w.wait(Y, "sS", w.notes[("cBv", ci - 4)])
                    lv = load(slot[:, 0:1024], matT[ci * 128:(ci + 1) * 128, 0:1024])
                    w.wait_t(S, lv)
                    if ci >= 4:
                        w.wait_t(S, w.notes[("cBst", ci - 4)])
                    lasts = sop(lambda s_, slot=slot, q=q: s_.activation(
                        K1[:, q * 1024:(q + 1) * 1024], slot[:, 0:1024], AF.Identity))
                    w.notes[("cBv", ci)] = lasts
                    w.notes[("cBst", ci)] = store(
                        scratch_q[ci * 128:(ci + 1) * 128, 0:1024],
                        K1[:, q * 1024:(q + 1) * 1024], after=("sS", lasts))

            conv_bf16(wvrT, wvrq, D)
            conv_bf16(wqrT, wqrq, D)
            conv_bf16(wvmT, wvmq, D)
            conv_bf16(wqmT, wqmq, D)
            deferred_convs = [lambda: conv_bf16(combT, combq, 3 * D)]
            if stage == "ffn1":
                deferred_convs = []
            for c in range(KC):
                w.wait(Y, "sV", w.c["sV"])
                w.wait(Y, "sS", w.c["sS"])
                lv = load(stg[0][:, 0:32], wzT[c * 128:(c + 1) * 128, 0:32])
                w.wait_t(V, lv)
                vop(lambda v, c=c: v.tensor_copy(WZ[:, c * 32:(c + 1) * 32], stg[0][:, 0:32]))
            lv = load(ROW[0][0:1, 0:32], zbrow)
            w.wait_t(T, lv)
            w.wait(T, "sV", w.c["sV"])
            w.bank_open("pb4")
            top(lambda t: MMSG(t, pb[4][:, 0:32], BC1[:], ROW[0][0:1, 0:32],
                                   start=w.bank_start("pb4"), stop=True))
            w.wait(V, "sT", w.c["sT"])
            vop(lambda v: v.tensor_copy(ZBB[:], pb[4][:, 0:32]))
            w.bank_consumed("pb4")
            wait_all_st(Y)
            wait_all_st(V)
            w.bar()

            class PieceStream:
                def __init__(self, srcs):
                    self.srcs, self.rdy, self.nxt = srcs, {}, 0
                    for _ in range(min(3, len(srcs))):
                        self._pf()

                def _pf(self):
                    j = self.nxt
                    if j >= len(self.srcs):
                        return
                    view3, ncol = self.srcs[j]
                    w.wait(Y, "sT", w.wb_lastuse[j % 3])
                    self.rdy[j] = load(sb3(WB[j % 3][:], ncol), view3)
                    self.nxt += 1

                def begin(self, j):
                    w.wait_t(T, self.rdy[j])
                    return WB[j % 3]

                def end(self, j):
                    w.wb_lastuse[j % 3] = w.c["sT"]
                    self._pf()

            def bcast(row_ap, dst):
                w.wait(T, "sV", w.c["sV"])
                w.bank_open("pb4")
                top(lambda t: MMSG(t, pb[4][:, 0:TC], BC1[:], row_ap[0:1, 0:TC],
                                       start=w.bank_start("pb4"), stop=True))
                w.wait(S, "sT", w.c["sT"])
                sop(lambda s: s.activation(dst[:, 0:TC], pb[4][:, 0:TC], AF.Identity))
                w.bank_consumed("pb4", "sS")
                w.wait(V, "sS", w.c["sS"])

            def maxabs_rows(src, nch, row_out):
                sl = src if callable(src) else (lambda c, b=src: b[:, c * TC:(c + 1) * TC])
                w.wait(S, "sV", w.c["sV"])
                for c in range(nch):
                    qb = SQ2 if c % 2 == 0 else SQ2B
                    if c >= 2:
                        w.wait(S, "sV", w.notes[("mxq", c - 2)])
                    sop(lambda s, c=c, qb=qb: s.activation(qb[:], sl(c), AF.Abs))
                    w.wait(V, "sS", w.c["sS"])
                    if c == 0:
                        vopd(lambda v, qb=qb: v.tensor_copy(MX1[:], qb[:]))
                    else:
                        vopd(lambda v, qb=qb: v.tensor_tensor(MX1[:], MX1[:], qb[:], ALU.max))
                    w.notes[("mxq", c)] = w.c["sV"]
                w.wait(T, "sV", w.c["sV"])
                w.bank_open("pstat")
                for j in range(NT):
                    top(lambda t, j=j: t.transpose(
                        pstat[:, j * 128:(j + 1) * 128], MX1[:, j * 128:(j + 1) * 128], IDF[:]))
                    if j == 0:
                        w.bank_started["pstat"] = True
                w.wait(V, "sT", w.c["sT"])
                for j in range(NT):
                    vop(lambda v, j=j: v.tensor_reduce(
                        MXT[:, j:j + 1], pstat[:, j * 128:(j + 1) * 128], AX.X, ALU.max))
                w.bank_consumed("pstat")
                w.wait(T, "sV", w.c["sV"])
                w.bank_open("pstat")
                for j in range(NT):
                    top(lambda t, j=j: t.transpose(
                        pstat[0:1, j * 128:(j + 1) * 128], MXT[:, j:j + 1], IDF[:]))
                w.wait(V, "sT", w.c["sT"])
                vopd(lambda v: v.tensor_copy(row_out[0:1, 0:TC], pstat[0:1, 0:TC]))
                w.bank_consumed("pstat")

            def maxabs_finalize(row_out):
                w.wait(T, "sV", w.c["sV"])
                w.bank_open("pstat")
                for j in range(NT):
                    top(lambda t, j=j: t.transpose(
                        pstat[:, j * 128:(j + 1) * 128], MX1[:, j * 128:(j + 1) * 128], IDF[:]))
                    if j == 0:
                        w.bank_started["pstat"] = True
                w.wait(V, "sT", w.c["sT"])
                for j in range(NT):
                    vop(lambda v, j=j: v.tensor_reduce(
                        MXT[:, j:j + 1], pstat[:, j * 128:(j + 1) * 128], AX.X, ALU.max))
                w.bank_consumed("pstat")
                w.wait(T, "sV", w.c["sV"])
                w.bank_open("pstat")
                for j in range(NT):
                    top(lambda t, j=j: t.transpose(
                        pstat[0:1, j * 128:(j + 1) * 128], MXT[:, j:j + 1], IDF[:]))
                w.wait(V, "sT", w.c["sT"])
                vopd(lambda v: v.tensor_copy(row_out[0:1, 0:TC], pstat[0:1, 0:TC]))
                w.bank_consumed("pstat")

            def emit_ln(src, dst, affine=False, add_to=None):
                w.bar()
                for c in range(KC):
                    qb = SQ2 if c % 2 == 0 else SQ2B
                    sop(lambda s, c=c, qb=qb: s.activation(qb[:], src[:, c * TC:(c + 1) * TC], AF.Square))
                    w.wait(T, "sS", w.c["sS"])
                    if c == 0:
                        w.bank_open("pstat")
                    top(lambda t, c=c: MMSG(t,
                        pstat[0:1, 0:TC], ONE[:], src[:, c * TC:(c + 1) * TC],
                        start=w.bank_start("pstat"), stop=(c == KC - 1)))
                    top(lambda t, c=c, qb=qb: MMSG(t,
                        pstat[32:33, 0:TC], ONE[:], qb[:],
                        start=(c == 0), stop=(c == KC - 1)))
                    w.notes[("lnq", c)] = w.c["sT"]
                    if c >= 1:
                        w.wait(S, "sT", w.notes[("lnq", c - 1)])
                w.wait(V, "sT", w.c["sT"])
                vopd(lambda v: v.tensor_scalar(ROW[0][:], pstat[0:1, 0:TC], 1.0 / D, None, ALU.mult))
                vop(lambda v: v.tensor_scalar(ROW[1][:], pstat[32:33, 0:TC], 1.0 / D, None, ALU.mult))
                w.bank_consumed("pstat")
                vopd(lambda v: v.tensor_tensor(ROW[2][:], ROW[0][:], ROW[0][:], ALU.mult))
                vopd(lambda v: v.tensor_sub(ROW[1][:], ROW[1][:], ROW[2][:]))
                vopd(lambda v: v.tensor_scalar_add(ROW[1][:], ROW[1][:], EPS))
                w.wait(S, "sV", w.c["sV"])
                sop(lambda s: s.activation(ROW[2][:], ROW[1][:], AF.Ln))
                sopd(lambda s: s.activation(ROW[2][:], ROW[2][:], AF.Exp, scale=CON[0:1, 2:3]))
                w.wait(V, "sS", w.c["sS"])
                vopd(lambda v: v.scalar_tensor_tensor(ROW[3][:], ROW[0][:], -1.0, ROW[2][:],
                                                     ALU.mult, ALU.mult))
                w.wait(T, "sV", w.c["sV"])
                w.bank_open("pb4")
                top(lambda t: MMSG(t, pb[4][:, 0:TC], BC1[:], ROW[2][0:1, 0:TC],
                                   start=w.bank_start("pb4"), stop=True))
                w.bank_open("pb5")
                top(lambda t: MMSG(t, pb[5][:, 0:TC], BC1[:], ROW[3][0:1, 0:TC],
                                   start=w.bank_start("pb5"), stop=True))
                w.wait(V, "sT", w.c["sT"])
                for c in range(KC):
                    f = vopd if c == 0 else vop
                    f(lambda v, c=c: v.tensor_tensor(
                        dst[:, c * TC:(c + 1) * TC], src[:, c * TC:(c + 1) * TC], pb[4][:, 0:TC], ALU.mult))
                for c in range(KC):
                    f = vopd if c == 0 else vop
                    f(lambda v, c=c: v.tensor_tensor(
                        dst[:, c * TC:(c + 1) * TC], dst[:, c * TC:(c + 1) * TC], pb[5][:, 0:TC], ALU.add))
                w.bank_consumed("pb4")
                w.bank_consumed("pb5")
                if affine:
                    for c in range(KC):
                        f = vopd if c == 0 else vop
                        f(lambda v, c=c: v.tensor_scalar(
                            dst[:, c * TC:(c + 1) * TC], dst[:, c * TC:(c + 1) * TC],
                            LNG[:, c:c + 1], LNB[:, c:c + 1], ALU.mult, ALU.add))
                if add_to is not None:
                    for c in range(KC):
                        f = vopd if c == 0 else vop
                        f(lambda v, c=c: v.tensor_tensor(
                            add_to[:, c * TC:(c + 1) * TC], add_to[:, c * TC:(c + 1) * TC],
                            dst[:, c * TC:(c + 1) * TC], ALU.add))
                w.bar()

            def emit_lnq(src):
                w.bar()
                w.bank_open("pb4")
                for c in range(KC):
                    qb = SQ2 if c % 2 == 0 else SQ2B
                    sop(lambda s, c=c, qb=qb: s.activation(
                        qb[:], src[:, c * TC:(c + 1) * TC], AF.Square))
                    w.wait(T, "sS", w.c["sS"])
                    top(lambda t, c=c: MMSG(t,
                        pb[4][0:1, 0:TC], ONE[:], src[:, c * TC:(c + 1) * TC],
                        start=w.bank_start("pb4"), stop=(c == KC - 1)))
                    top(lambda t, c=c, qb=qb: MMSG(t,
                        pb[4][32:33, 0:TC], ONE[:], qb[:],
                        start=(c == 0), stop=(c == KC - 1)))
                    w.notes[("lnq", c)] = w.c["sT"]
                    if c >= 1:
                        w.wait(S, "sT", w.notes[("lnq", c - 1)])
                sl = lambda c: src[:, c * TC:(c + 1) * TC]
                qs = lambda q: QSCW[:, q * TC:(q + 1) * TC]
                for (acc, op) in ((MX1, ALU.max), (BRD, ALU.min)):
                    for q in range(4):
                        f = vopd if q == 0 else vop
                        f(lambda v, q=q, op=op: v.tensor_tensor(
                            qs(q), sl(2 * q), sl(2 * q + 1), op))
                    vopd(lambda v, op=op: v.tensor_tensor(qs(0), qs(0), qs(1), op))
                    vop(lambda v, op=op: v.tensor_tensor(qs(2), qs(2), qs(3), op))
                    vopd(lambda v, acc=acc, op=op: v.tensor_tensor(
                        acc[:, 0:TC], qs(0), qs(2), op))
                w.wait(T, "sV", w.c["sV"])
                w.bank_open("pstat")
                for j in range(NT):
                    top(lambda t, j=j: t.transpose(
                        pstat[:, j * 128:(j + 1) * 128], MX1[:, j * 128:(j + 1) * 128], IDF[:]))
                    if j == 0:
                        w.bank_started["pstat"] = True
                w.wait(V, "sT", w.c["sT"])
                for j in range(NT):
                    vop(lambda v, j=j: v.tensor_reduce(
                        MXT[:, 2 * j:2 * j + 1], pstat[:, j * 128:(j + 1) * 128], AX.X, ALU.max))
                w.bank_consumed("pstat")
                w.wait(T, "sV", w.c["sV"])
                w.bank_open("pstat")
                for j in range(NT):
                    top(lambda t, j=j: t.transpose(
                        pstat[:, j * 128:(j + 1) * 128], BRD[:, j * 128:(j + 1) * 128], IDF[:]))
                    if j == 0:
                        w.bank_started["pstat"] = True
                w.wait(V, "sT", w.c["sT"])
                for j in range(NT):
                    vop(lambda v, j=j: v.tensor_reduce(
                        MXT[:, 2 * j + 1:2 * j + 2], pstat[:, j * 128:(j + 1) * 128], AX.X, ALU.min))
                w.bank_consumed("pstat")
                w.wait(T, "sV", w.c["sV"])
                w.bank_open("pstat")
                w.bank_open("pb5")
                for j in range(NT):
                    top(lambda t, j=j: t.transpose(
                        pstat[0:1, j * 128:(j + 1) * 128], MXT[:, 2 * j:2 * j + 1], IDF[:]))
                    if j == 0:
                        w.bank_started["pstat"] = True
                        w.bank_started["pb5"] = True
                    top(lambda t, j=j: t.transpose(
                        pb[5][0:1, j * 128:(j + 1) * 128], MXT[:, 2 * j + 1:2 * j + 2], IDF[:]))
                w.wait(V, "sT", w.c["sT"])
                vopd(lambda v: v.tensor_copy(ROW[4][:], pstat[0:1, 0:TC]))
                vop(lambda v: v.tensor_copy(ROW[3][:], pb[5][0:1, 0:TC]))
                w.bank_consumed("pstat")
                w.bank_consumed("pb5")
                vopd(lambda v: v.tensor_scalar(ROW[0][:], pb[4][0:1, 0:TC], 1.0 / D, None, ALU.mult))
                vop(lambda v: v.tensor_scalar(ROW[1][:], pb[4][32:33, 0:TC], 1.0 / D, None, ALU.mult))
                w.bank_consumed("pb4")
                vopd(lambda v: v.tensor_tensor(ROW[2][:], ROW[0][:], ROW[0][:], ALU.mult))
                vopd(lambda v: v.tensor_sub(ROW[1][:], ROW[1][:], ROW[2][:]))
                vopd(lambda v: v.tensor_scalar_add(ROW[1][:], ROW[1][:], EPS))
                w.wait(S, "sV", w.c["sV"])
                sop(lambda s: s.activation(ROW[2][:], ROW[1][:], AF.Ln))
                sopd(lambda s: s.activation(ROW[2][:], ROW[2][:], AF.Exp, scale=CON[0:1, 2:3]))
                vopd(lambda v: v.tensor_sub(ROW[4][:], ROW[4][:], ROW[0][:]))
                vop(lambda v: v.tensor_sub(ROW[3][:], ROW[0][:], ROW[3][:]))
                vopd(lambda v: v.tensor_tensor(ROW[4][:], ROW[4][:], ROW[3][:], ALU.max))
                w.wait(V, "sS", w.c["sS"])
                vopd(lambda v: v.tensor_tensor(ROW[4][:], ROW[4][:], ROW[2][:], ALU.mult))
                vopd(lambda v: v.tensor_scalar(ROW[4][:], ROW[4][:], 1e-5, None, ALU.max))
                vopd(lambda v: v.reciprocal(ROW[1][:], ROW[4][:]))
                vopd(lambda v: v.tensor_scalar(ROW[1][:], ROW[1][:], 127.0, None, ALU.mult))
                vopd(lambda v: v.tensor_tensor(ROW[2][:], ROW[2][:], ROW[1][:], ALU.mult))
                vopd(lambda v: v.scalar_tensor_tensor(ROW[3][:], ROW[0][:], -1.0, ROW[2][:],
                                                      ALU.mult, ALU.mult))
                bcast(ROW[2], BRD)
                bcast(ROW[3], BRDG)
                srcs = []
                for p in range(8):
                    srcs.append((fmview(wgq, 0, D, p * 512, 512), 512))
                    srcs.append((fmview(wuq, 0, D, p * 512, 512), 512))
                PS = PieceStream(srcs)
                wait_all_st(V)
                for g4 in range(KC // 4):
                    for q in range(4):
                        c = g4 * 4 + q
                        f = vopd if q == 0 else vop
                        f(lambda v, c=c, q=q: v.tensor_tensor(
                            QSCW[:, q * TC:(q + 1) * TC], src[:, c * TC:(c + 1) * TC],
                            BRD[:], ALU.mult))
                    for q in range(4):
                        f = vopd if q == 0 else vop
                        f(lambda v, q=q: v.tensor_tensor(
                            QSCW[:, q * TC:(q + 1) * TC], QSCW[:, q * TC:(q + 1) * TC],
                            BRDG[:], ALU.add))
                    for q in range(4):
                        c = g4 * 4 + q
                        f = vopd if q == 0 else vop
                        f(lambda v, c=c, q=q: v.tensor_scalar(
                            K1[:, c * TC:(c + 1) * TC], QSCW[:, q * TC:(q + 1) * TC],
                            RMAGIC, -RMAGIC, ALU.add, ALU.add))
                return PS

            K2 = [RVB, RQB, MVB, MQB]

            def k2sl(ic):
                return K2[ic // 8][:, (ic % 8) * TC:(ic % 8 + 1) * TC]

            def mid_sl(ic):
                if ic < KC:
                    return ARN[:, ic * TC:(ic + 1) * TC]
                return MIDF[:, (ic - KC) * TC:(ic - KC + 1) * TC]

            def emit_ffn(PS):
                wait_all_st(S)
                wait_all_st(V)
                vopd(lambda v: v.tensor_scalar(ROW[3][:], ROW[4][:], MW[0:1, 4:5], None, ALU.mult))
                bcast(ROW[3], BRDG)
                w.bar()
                for p in range(8):
                    bufg = PS.begin(2 * p)
                    bufu = None
                    for icl in range(4):
                        ic = p * 4 + icl
                        rg3 = ic % 3
                        bg, bu = pb[rg3], pb[3 + rg3]
                        w.bank_open(f"pb{rg3}")
                        for c in range(KC):
                            top(lambda t, c=c, icl=icl, bufg=bufg, bg=bg: MMSG(t,
                                bg[:, 0:TC],
                                bufg[:, c * 512 + icl * 128: c * 512 + (icl + 1) * 128],
                                K1[:, c * TC:(c + 1) * TC],
                                start=w.bank_start(f"pb{rg3}") if c == 0 else False,
                                stop=(c == KC - 1)))
                        if icl == 0:
                            bufu = PS.begin(2 * p + 1)
                        w.bank_open(f"pb{3 + rg3}")
                        for c in range(KC):
                            top(lambda t, c=c, icl=icl, bufu=bufu, bu=bu: MMSG(t, 
                                bu[:, 0:TC],
                                bufu[:, c * 512 + icl * 128: c * 512 + (icl + 1) * 128],
                                K1[:, c * TC:(c + 1) * TC],
                                start=w.bank_start(f"pb{3 + rg3}") if c == 0 else False,
                                stop=(c == KC - 1)))
                        w.wait(V, "sT", w.c["sT"])
                        vopd(lambda v, bg=bg: v.tensor_tensor(SQ2[:], bg[:, 0:TC], BRDG[:], ALU.mult))
                        w.bank_consumed(f"pb{rg3}")
                        w.wait(S, "sV", w.c["sV"])
                        sop(lambda s: s.activation(SQ2[:], SQ2[:], AF.Silu))
                        w.wait(V, "sS", w.c["sS"])
                        vop(lambda v, ic=ic, bu=bu: v.tensor_tensor(
                            mid_sl(ic), SQ2[:], bu[:, 0:TC], ALU.mult))
                        w.bank_consumed(f"pb{3 + rg3}")
                        qb = SQ2B if ic % 2 == 0 else SQ2
                        w.wait(S, "sV", w.c["sV"])
                        sop(lambda s, ic=ic, qb=qb: s.activation(qb[:], mid_sl(ic), AF.Abs))
                        w.wait(V, "sS", w.c["sS"])
                        if ic == 0:
                            vopd(lambda v, qb=qb: v.tensor_copy(MX1[:], qb[:]))
                        else:
                            vopd(lambda v, qb=qb: v.tensor_tensor(MX1[:], MX1[:], qb[:], ALU.max))
                    PS.end(2 * p)
                    PS.end(2 * p + 1)
                srcsd = [(fmview(wdq, p * 512, 512, 0, 1024), 1024) for p in range(8)]
                PSd = PieceStream(srcsd + srcsd)
                maxabs_finalize(ROW[2])
                vopd(lambda v: v.tensor_scalar(ROW[2][:], ROW[2][:], 1e-5, None, ALU.max))
                vopd(lambda v: v.reciprocal(ROW[1][:], ROW[2][:]))
                vopd(lambda v: v.tensor_scalar(ROW[1][:], ROW[1][:], 127.0, None, ALU.mult))
                bcast(ROW[1], BRD)
                vopd(lambda v: v.tensor_tensor(ROW[3][:], ROW[2][:], ROW[4][:], ALU.mult))
                vopd(lambda v: v.tensor_scalar(ROW[3][:], ROW[3][:], MW[0:1, 3:4], None, ALU.mult))
                bcast(ROW[3], BRDG)
                w.bar()
                for h in range(2):
                    for dl in range(4):
                        w.bank_open(f"pb{dl}")
                    for p in range(8):
                        if h == 0:
                            for q in range(4):
                                ic = p * 4 + q
                                f = vopd if q == 0 else vop
                                f(lambda v, ic=ic, q=q: v.tensor_tensor(
                                    QSCW[:, q * TC:(q + 1) * TC], mid_sl(ic), BRD[:], ALU.mult))
                            for q in range(4):
                                ic = p * 4 + q
                                f = vopd if q == 0 else vop
                                f(lambda v, ic=ic, q=q: v.tensor_scalar(
                                    k2sl(ic), QSCW[:, q * TC:(q + 1) * TC],
                                    RMAGIC, -RMAGIC, ALU.add, ALU.add))
                            w.wait(T, "sV", w.c["sV"])
                        buf = PSd.begin(h * 8 + p)
                        for kcl in range(4):
                            ic = p * 4 + kcl
                            for dl in range(4):
                                dout = h * 4 + dl
                                top(lambda t, kcl=kcl, dout=dout, dl=dl, buf=buf, ic=ic: MMSG(t, 
                                    pb[dl][:, 0:TC],
                                    buf[:, kcl * 1024 + dout * 128: kcl * 1024 + (dout + 1) * 128],
                                    k2sl(ic),
                                    start=w.bank_start(f"pb{dl}") if (p == 0 and kcl == 0) else False,
                                    stop=(p == 7 and kcl == 3)))
                        PSd.end(h * 8 + p)
                    w.wait(V, "sT", w.c["sT"])
                    for dl in range(4):
                        f = vopd if dl == 0 else vop
                        f(lambda v, dl=dl: v.tensor_tensor(
                            QSCW[:, dl * TC:(dl + 1) * TC], pb[dl][:, 0:TC], BRDG[:], ALU.mult))
                    for dl in range(4):
                        dout = h * 4 + dl
                        f = vopd if dl == 0 else vop
                        f(lambda v, dout=dout, dl=dl: v.tensor_tensor(
                            R[:, dout * TC:(dout + 1) * TC], R[:, dout * TC:(dout + 1) * TC],
                            QSCW[:, dl * TC:(dl + 1) * TC], ALU.add))
                    for dl in range(4):
                        w.bank_consumed(f"pb{dl}")
                w.bar()

            def emit_zab(decay_global=None):
                ZS = QSCW
                w.wait(T, "sV", w.c["sV"])
                w.wait(T, "sS", w.c["sS"])
                w.bank_open("pstat")
                for j in range(NT):
                    for c in range(KC):
                        top(lambda t, c=c, j=j: MMSG(t,
                            pstat[:, 256 + j * 32: 256 + (j + 1) * 32],
                            Y2B[:, c * TC + j * 128: c * TC + (j + 1) * 128],
                            WZ[:, c * 32:(c + 1) * 32],
                            start=w.bank_start("pstat") if (j == 0 and c == 0) else False,
                            stop=(c == KC - 1)))
                w.wait(V, "sT", w.c["sT"])
                for j in range(NT):
                    f = vopd if j == 0 else vop
                    f(lambda v, j=j: v.tensor_tensor(
                        ZS[:, j * 32:(j + 1) * 32], pstat[:, 256 + j * 32: 256 + (j + 1) * 32],
                        ZBB[:], ALU.add))
                w.bank_consumed("pstat")
                w.wait(S, "sV", w.c["sV"])
                zview = ZS[:, 0:NT * 32].rearrange("p (t z) -> p t z", z=32)
                gview = GT8[:, 0:2 * NT].rearrange("p (t z) -> p t z", z=2)
                sop(lambda s: s.activation(gview, zview[:, :, 24:26], AF.Exp, scale=CON[:, 1:2]))
                w.wait(V, "sS", w.c["sS"])
                vopd(lambda v: v.tensor_scalar_add(GT8[:, 0:2 * NT], GT8[:, 0:2 * NT], 1.0))
                vopd(lambda v: v.reciprocal(GT8[:, 0:2 * NT], GT8[:, 0:2 * NT]))
                segs = []
                for j in range(NT):
                    segs.append((j * 32, NREG, 2 * j))
                    segs.append((j * 32 + 8, NMEM, 2 * j + 1))
                w.do(V, lambda v: v.drain())
                for (c0, n, k) in segs:
                    vop(lambda v, c0=c0, n=n, k=k: v.tensor_reduce(
                        SMX[:, 8 + k: 9 + k], ZS[:, c0:c0 + n], AX.X, ALU.max, negate=True))
                w.wait(S, "sV", w.c["sV"])
                for (c0, n, k) in segs:
                    sop(lambda s, c0=c0, n=n, k=k: s.activation(
                        ZS[:, c0:c0 + n], ZS[:, c0:c0 + n], AF.Exp, bias=SMX[:, 8 + k: 9 + k]))
                w.wait(V, "sS", w.c["sS"])
                for (c0, n, k) in segs:
                    vop(lambda v, c0=c0, n=n, k=k: v.tensor_reduce(
                        SMX[:, 16 + k: 17 + k], ZS[:, c0:c0 + n], AX.X, ALU.add))
                w.do(V, lambda v: v.drain())
                for (c0, n, k) in segs:
                    vop(lambda v, k=k: v.reciprocal(SMX[:, 16 + k: 17 + k], SMX[:, 16 + k: 17 + k]))
                w.do(V, lambda v: v.drain())
                for (c0, n, k) in segs:
                    vop(lambda v, c0=c0, n=n, k=k: v.tensor_scalar(
                        ZS[:, c0:c0 + n], ZS[:, c0:c0 + n], SMX[:, 16 + k: 17 + k], None, ALU.mult))
                w.do(V, lambda v: v.drain())
                for j in range(NT):
                    vop(lambda v, j=j: v.tensor_scalar(
                        ABT[:, j * 32: j * 32 + 8], ZS[:, j * 32: j * 32 + 8],
                        GT8[:, 2 * j: 2 * j + 1], None, ALU.mult))
                    vop(lambda v, j=j: v.tensor_scalar(
                        ABT[:, j * 32 + 8: j * 32 + 24], ZS[:, j * 32 + 8: j * 32 + 24],
                        GT8[:, 2 * j + 1: 2 * j + 2], None, ALU.mult))
                if decay_global is not None:
                    w.do(V, lambda v: v.drain())
                    for j in range(NT):
                        vop(lambda v, j=j, gci=decay_global[j]: v.tensor_scalar(
                            ABT[:, j * 32 + 8: j * 32 + 24], ABT[:, j * 32 + 8: j * 32 + 24],
                            DVF[:, gci:gci + 1], None, ALU.mult))
                w.wait(S, "sV", w.c["sV"])
                for j in range(NT):
                    sop(lambda s_, j=j: s_.activation(
                        ABF[:, j * 32: j * 32 + 24], ABT[:, j * 32: j * 32 + 24],
                        AF.Identity))

            def precompute_srcs():
                srcs = []
                for mq_ in (wvrq, wqrq, wvmq, wqmq):
                    srcs.append((fmview(mq_, 0, D, 0, 512), 512))
                    srcs.append((fmview(mq_, 0, D, 512, 512), 512))
                return srcs

            def emit_precompute(PS):
                mats = [(wvrq, RVB, BVr), (wqrq, RQB, BQr), (wvmq, MVB, BVm), (wqmq, MQB, BQm)]
                w.wait(T, "sV", w.c["sV"])
                w.wait(T, "sS", w.c["sS"])
                for mi, (mq_, dstb, bias) in enumerate(mats):
                    for hhalf in range(2):
                        buf = PS.begin(mi * 2 + hhalf)
                        for dl in range(4):
                            dout = hhalf * 4 + dl
                            bank = f"pb{dl % 2}"
                            w.bank_open(bank)
                            for c in range(KC):
                                top(lambda t, c=c, dl=dl, buf=buf: MMSG(t, 
                                    pb[dl % 2][:, 0:TC],
                                    buf[:, c * 512 + dl * 128: c * 512 + (dl + 1) * 128],
                                    Y2B[:, c * TC:(c + 1) * TC],
                                    start=w.bank_start(bank) if c == 0 else False,
                                    stop=(c == KC - 1)))
                            w.wait(S, "sT", w.c["sT"])
                            sop(lambda s, dout=dout, dstb=dstb, bias=bias, dl=dl: s.activation(
                                dstb[:, dout * TC:(dout + 1) * TC], pb[dl % 2][:, 0:TC],
                                AF.Identity, bias=bias[:, dout:dout + 1]))
                            w.bank_consumed(bank, "sS")
                        PS.end(mi * 2 + hhalf)
                w.bar()

            def emit_scan_chunk(j, mode="use", g=0):
                a_sl = ABF[:, j * 32: j * 32 + 8]
                b_sl = ABF[:, j * 32 + 8: j * 32 + 24]

                def ch(buf, c):
                    return buf[:, c * TC + j * 128: c * TC + (j + 1) * 128]

                if mode == "agg":
                    w.wait(T, "sV", w.c["sV"])
                    w.wait(T, "sS", w.c["sS"])
                    w.bank_open("ptr")
                    for c in range(KC):
                        top(lambda t, c=c: t.transpose(
                            ptr[:, c * 128:(c + 1) * 128], ch(RVB, c), IDN[:]))
                        if c == 0:
                            w.bank_started["ptr"] = True
                    w.wait(S, "sT", w.c["sT"])
                    for c in range(KC):
                        sop(lambda s_, c=c: s_.activation(
                            VrTM[:, c * 128:(c + 1) * 128],
                            ptr[:, c * 128:(c + 1) * 128], AF.Identity))
                    w.bank_consumed("ptr", "sS")
                    w.wait(T, "sS", w.c["sS"])
                    w.bank_open("ptr")
                    for c in range(KC):
                        top(lambda t, c=c: t.transpose(
                            ptr[:, c * 128:(c + 1) * 128], ch(MVB, c), IDN[:]))
                    w.wait(S, "sT", w.c["sT"])
                    for c in range(KC):
                        sop(lambda s_, c=c: s_.activation(
                            VmTM[:, c * 128:(c + 1) * 128],
                            ptr[:, c * 128:(c + 1) * 128], AF.Identity))
                    w.bank_consumed("ptr", "sS")
                    vopd(lambda v: v.tensor_scalar(BDCB[:], ABT[:, j * 32 + 8: j * 32 + 24],
                                                   DDV[:, 0:1], None, ALU.mult))
                    w.wait(T, "sV", w.c["sV"])
                    w.wait(T, "sS", w.c["sS"])
                    w.bank_open("pstat")
                    for c in range(KC):
                        top(lambda t, c=c: MMSG(t,
                            pstat[:, 64 + c * 8: 64 + (c + 1) * 8],
                            VrTM[:, c * 128:(c + 1) * 128], a_sl,
                            start=w.bank_start("pstat") if c == 0 else False, stop=True))
                    for c in range(KC):
                        top(lambda t, c=c: MMSG(t,
                            pstat[:, 192 + c * 16: 192 + (c + 1) * 16],
                            VmTM[:, c * 128:(c + 1) * 128], BDCB[:],
                            start=False, stop=True))
                    w.wait(V, "sT", w.c["sT"])
                    w.wait(V, "sS", w.c["sS"])
                    vopd(lambda v: v.tensor_tensor(SRF[:], SRF[:], pstat[:, 64:128], ALU.add))
                    vop(lambda v: v.tensor_scalar(SMF[:], SMF[:], DEC128, None, ALU.mult))
                    vopd(lambda v: v.tensor_tensor(SMF[:], SMF[:], pstat[:, 192:320], ALU.add))
                    w.bank_consumed("pstat")
                    w.wait(S, "sV", w.c["sV"])
                    sop(lambda s_, j=j: s_.activation(
                        SLOC[:, j * 192: j * 192 + 64], SRF[:], AF.Identity))
                    sop(lambda s_, j=j: s_.activation(
                        SLOC[:, j * 192 + 64: (j + 1) * 192], SMF[:], AF.Identity))
                    return

                w.wait(V, "sS", w.c["sS"])
                if g == 0:
                    vopd(lambda v: v.tensor_copy(SRF[:], SINI[:, 0:64]))
                    vop(lambda v: v.tensor_copy(SMF[:], SINI[:, 64:192]))
                else:
                    dpow = float(DECAY ** (128 * g))
                    vopd(lambda v, j=j: v.tensor_tensor(
                        SRF[:], SINI[:, 0:64], SLOC[:, j * 192: j * 192 + 64], ALU.add))
                    vop(lambda v, j=j, dpow=dpow: v.scalar_tensor_tensor(
                        SMF[:], SINI[:, 64:192], dpow,
                        SLOC[:, j * 192 + 64: (j + 1) * 192], ALU.mult, ALU.add))
                w.wait(S, "sV", w.c["sV"])
                sop(lambda s_: s_.activation(SRFB[:], SRF[:], AF.Identity))
                sop(lambda s_: s_.activation(SMFB[:], SMF[:], AF.Identity))

                w.wait(T, "sV", w.c["sV"])
                w.wait(T, "sS", w.c["sS"])
                w.bank_open("ptr")
                for c in range(KC):
                    top(lambda t, c=c: t.transpose(
                        ptr[0:8, c * 128:(c + 1) * 128], SRFB[:, c * 8:(c + 1) * 8], IDN[:]))
                    if c == 0:
                        w.bank_started["ptr"] = True
                w.wait(S, "sT", w.c["sT"])
                for c in range(KC):
                    sop(lambda s_, c=c: s_.activation(
                        SRNB[0:8, c * 128:(c + 1) * 128], ptr[0:8, c * 128:(c + 1) * 128],
                        AF.Identity))
                w.bank_consumed("ptr", "sS")
                w.wait(T, "sS", w.c["sS"])
                w.bank_open("ptr")
                for c in range(KC):
                    top(lambda t, c=c: t.transpose(
                        ptr[0:16, c * 128:(c + 1) * 128], SMFB[:, c * 16:(c + 1) * 16], IDN[:]))
                w.wait(S, "sT", w.c["sT"])
                for c in range(KC):
                    sop(lambda s_, c=c: s_.activation(
                        SMNB[0:16, c * 128:(c + 1) * 128], ptr[0:16, c * 128:(c + 1) * 128],
                        AF.Identity))
                w.bank_consumed("ptr", "sS")
                w.bank_open("pb5")
                for c in range(KC):
                    top(lambda t, c=c: MMSG(t, 
                        pb[5][:, 0:128], ch(RVB, c), ch(RQB, c),
                        start=w.bank_start("pb5") if c == 0 else False, stop=(c == KC - 1)))
                for c in range(KC):
                    top(lambda t, c=c: MMSG(t, 
                        pb[5][:, 128:256], ch(MVB, c), ch(MQB, c),
                        start=False, stop=(c == KC - 1)))
                w.wait(V, "sT", w.c["sT"])
                vop(lambda v: v.tensor_tensor(P1B[:, 0:128], pb[5][:, 0:128], MKUL[:], ALU.mult))
                vop(lambda v: v.tensor_tensor(P1B[:, 128:256], pb[5][:, 128:256], MKDC[:], ALU.mult))
                w.wait(T, "sV", w.c["sV"])
                w.wait(T, "sS", w.c["sS"])
                w.bank_open("pstat")
                for c in range(KC):
                    top(lambda t, c=c: MMSG(t, 
                        pstat[:, 0:8], ch(RQB, c), SRFB[:, c * 8:(c + 1) * 8],
                        start=w.bank_start("pstat") if c == 0 else False, stop=False))
                top(lambda t: MMSG(t, pstat[:, 0:8], P1B[:, 0:128], a_sl,
                                       start=False, stop=True))
                for c in range(KC):
                    top(lambda t, c=c: MMSG(t, 
                        pstat[:, 16:32], ch(MQB, c), SMFB[:, c * 16:(c + 1) * 16],
                        start=False, stop=(c == KC - 1)))
                top(lambda t: MMSG(t, pstat[:, 32:48], P1B[:, 128:256], b_sl,
                                       start=False, stop=True))
                w.wait(V, "sT", w.c["sT"])
                vopd(lambda v: v.tensor_reduce(SMX[:, 0:1], pstat[:, 0:8], AX.X, ALU.max, negate=True))
                vopd(lambda v: v.tensor_scalar(SMX[:, 1:2], SMX[:, 0:1], 0.03125, None, ALU.mult))
                w.wait(S, "sV", w.c["sV"])
                sop(lambda s: s.activation(PR[:], pstat[:, 0:8], AF.Exp,
                                           bias=SMX[:, 1:2], scale=CON[:, 0:1]))
                w.wait(V, "sS", w.c["sS"])
                vop(lambda v: v.tensor_reduce(SMX[:, 2:3], PR[:], AX.X, ALU.add))
                vopd(lambda v: v.reciprocal(SMX[:, 2:3], SMX[:, 2:3]))
                vopd(lambda v: v.tensor_scalar(PR[:], PR[:], SMX[:, 2:3], None, ALU.mult))
                w.wait(S, "sV", w.c["sV"])
                sop(lambda s_: s_.activation(PRB[:], PR[:], AF.Identity))
                vop(lambda v: v.tensor_copy(SCM[:], pstat[:, 32:48]))
                vopd(lambda v: v.scalar_tensor_tensor(SCM[:], pstat[:, 16:32], LAM[:, 0:1],
                                                     SCM[:], ALU.mult, ALU.add))
                w.bank_consumed("pstat")
                vopd(lambda v: v.tensor_reduce(SMX[:, 3:4], SCM[:], AX.X, ALU.max, negate=True))
                vopd(lambda v: v.tensor_scalar(SMX[:, 4:5], SMX[:, 3:4], 0.03125, None, ALU.mult))
                w.wait(S, "sV", w.c["sV"])
                sop(lambda s: s.activation(PM[:], SCM[:], AF.Exp,
                                           bias=SMX[:, 4:5], scale=CON[:, 0:1]))
                w.wait(V, "sS", w.c["sS"])
                vop(lambda v: v.tensor_reduce(SMX[:, 5:6], PM[:], AX.X, ALU.add))
                vopd(lambda v: v.reciprocal(SMX[:, 5:6], SMX[:, 5:6]))
                vopd(lambda v: v.tensor_scalar(PM[:], PM[:], SMX[:, 5:6], None, ALU.mult))
                vopd(lambda v: v.tensor_scalar(PM[:], PM[:], LAM[:, 0:1], None, ALU.mult))
                w.wait(S, "sV", w.c["sV"])
                sop(lambda s_: s_.activation(PMB[:], PM[:], AF.Identity))
                w.wait(T, "sV", w.c["sV"])
                w.wait(T, "sS", w.c["sS"])
                w.bank_open("ptr")
                top(lambda t: t.transpose(ptr[0:8, 0:128], a_sl, IDN[:]))
                w.bank_started["ptr"] = True
                top(lambda t: t.transpose(ptr[0:16, 128:256], b_sl, IDN[:]))
                top(lambda t: t.transpose(ptr[0:8, 256:384], PRB[:], IDN[:]))
                top(lambda t: t.transpose(ptr[0:16, 384:512], PMB[:], IDN[:]))
                w.wait(S, "sT", w.c["sT"])
                sop(lambda s_: s_.activation(TRS[0:8, 0:128], ptr[0:8, 0:128], AF.Identity))
                sop(lambda s_: s_.activation(TRS[0:16, 128:256], ptr[0:16, 128:256], AF.Identity))
                sop(lambda s_: s_.activation(TRS[0:8, 256:384], ptr[0:8, 256:384], AF.Identity))
                sop(lambda s_: s_.activation(TRS[0:16, 384:512], ptr[0:16, 384:512], AF.Identity))
                w.bank_consumed("ptr", "sS")
                w.wait(T, "sV", w.c["sV"])
                w.wait(T, "sS", w.c["sS"])
                w.bank_open("pb5")
                top(lambda t: MMSG(t, pb[5][:, 256:384], TRS[0:8, 0:128], TRS[0:8, 256:384],
                                       start=w.bank_start("pb5"), stop=True))
                top(lambda t: MMSG(t, pb[5][:, 384:512], TRS[0:16, 128:256], TRS[0:16, 384:512],
                                       start=False, stop=True))
                w.wait(V, "sT", w.c["sT"])
                vop(lambda v: v.tensor_tensor(EMB[:, 0:128], pb[5][:, 256:384], MKUL[:], ALU.mult))
                vop(lambda v: v.tensor_tensor(EMB[:, 128:256], pb[5][:, 384:512], MKDE[:], ALU.mult))
                w.bank_consumed("pb5")
                w.wait(T, "sS", w.c["sS"])
                w.bank_open("ptr")
                for c in range(KC):
                    top(lambda t, c=c: t.transpose(
                        ptr[:, c * 128:(c + 1) * 128], ch(RVB, c), IDN[:]))
                    if c == 0:
                        w.bank_started["ptr"] = True
                w.wait(S, "sT", w.c["sT"])
                for c in range(KC):
                    sop(lambda s_, c=c: s_.activation(
                        VrTM[:, c * 128:(c + 1) * 128], ptr[:, c * 128:(c + 1) * 128],
                        AF.Identity))
                w.bank_consumed("ptr", "sS")
                w.wait(T, "sS", w.c["sS"])
                w.bank_open("ptr")
                for c in range(KC):
                    top(lambda t, c=c: t.transpose(
                        ptr[:, c * 128:(c + 1) * 128], ch(MVB, c), IDN[:]))
                w.wait(S, "sT", w.c["sT"])
                for c in range(KC):
                    sop(lambda s_, c=c: s_.activation(
                        VmTM[:, c * 128:(c + 1) * 128], ptr[:, c * 128:(c + 1) * 128],
                        AF.Identity))
                w.bank_consumed("ptr", "sS")
                w.wait(T, "sV", w.c["sV"])
                w.wait(T, "sS", w.c["sS"])
                w.bank_open("pb0")
                w.bank_open("pb1")
                for c in range(KC):
                    bank = pb[0] if c < 4 else pb[1]
                    bname = "pb0" if c < 4 else "pb1"
                    col = (c % 4) * 128
                    top(lambda t, c=c, bank=bank, col=col, bname=bname: MMSG(t, 
                        bank[:, col:col + 128], SRNB[0:8, c * 128:(c + 1) * 128],
                        TRS[0:8, 256:384],
                        start=w.bank_start(bname), stop=False))
                    top(lambda t, c=c, bank=bank, col=col: MMSG(t, 
                        bank[:, col:col + 128], VrTM[:, c * 128:(c + 1) * 128],
                        EMB[:, 0:128], start=False, stop=True))
                w.wait(S, "sT", w.c["sT"])
                for c in range(KC):
                    bank = pb[0] if c < 4 else pb[1]
                    col = (c % 4) * 128
                    sop(lambda s_, c=c, bank=bank, col=col: s_.activation(
                        ch(RRD, c), bank[:, col:col + 128], AF.Identity))
                w.bank_consumed("pb0", "sS")
                w.bank_consumed("pb1", "sS")
                w.bank_open("pb2")
                w.bank_open("pb3")
                for c in range(KC):
                    bank = pb[2] if c < 4 else pb[3]
                    bname = "pb2" if c < 4 else "pb3"
                    col = (c % 4) * 128
                    top(lambda t, c=c, bank=bank, col=col, bname=bname: MMSG(t, 
                        bank[:, col:col + 128], SMNB[0:16, c * 128:(c + 1) * 128],
                        TRS[0:16, 384:512],
                        start=w.bank_start(bname), stop=False))
                    top(lambda t, c=c, bank=bank, col=col: MMSG(t, 
                        bank[:, col:col + 128], VmTM[:, c * 128:(c + 1) * 128],
                        EMB[:, 128:256], start=False, stop=True))
                w.wait(S, "sT", w.c["sT"])
                for c in range(KC):
                    bank = pb[2] if c < 4 else pb[3]
                    col = (c % 4) * 128
                    sop(lambda s_, c=c, bank=bank, col=col: s_.activation(
                        ch(MRD, c), bank[:, col:col + 128], AF.Identity))
                w.bank_consumed("pb2", "sS")
                w.bank_consumed("pb3", "sS")
            def emit_epilogue():
                srcs = []
                for h2 in range(2):
                    for m0 in range(3):
                        srcs.append((fmview(combq, m0 * D, D, h2 * 512, 512), 512))
                rhs_of = [Y2B, RRD, MRD]
                wait_all_st(Y)
                PS = PieceStream(srcs)
                w.wait(T, "sV", w.c["sV"])
                w.wait(T, "sS", w.c["sS"])
                for h2 in range(2):
                    for dl in range(4):
                        w.bank_open(f"pb{dl}")
                    for m0 in range(3):
                        buf = PS.begin(h2 * 3 + m0)
                        for dl in range(4):
                            bname = f"pb{dl}"
                            for c in range(KC):
                                top(lambda t, c=c, dl=dl, buf=buf, m0=m0: MMSG(t,
                                    pb[dl][:, 0:TC],
                                    buf[:, c * 512 + dl * 128: c * 512 + (dl + 1) * 128],
                                    rhs_of[m0][:, c * TC:(c + 1) * TC],
                                    start=w.bank_start(bname) if (m0 == 0 and c == 0) else False,
                                    stop=(m0 == 2 and c == KC - 1)))
                        PS.end(h2 * 3 + m0)
                    w.wait(S, "sT", w.c["sT"])
                    for dl in range(4):
                        dout = h2 * 4 + dl
                        sop(lambda s_, dout=dout, dl=dl: s_.activation(
                            ARN[:, dout * TC:(dout + 1) * TC], pb[dl][:, 0:TC],
                            AF.Identity, bias=CBB[:, dout:dout + 1]))
                    w.wait(V, "sS", w.c["sS"])
                    for dl in range(4):
                        dout = h2 * 4 + dl
                        f = vopd if dl == 0 else vop
                        f(lambda v, dout=dout: v.tensor_tensor(
                            ARN[:, dout * TC:(dout + 1) * TC], ARN[:, dout * TC:(dout + 1) * TC],
                            Y2F[:, dout * TC:(dout + 1) * TC], ALU.add))
                    for dl in range(4):
                        w.bank_consumed(f"pb{dl}", "sS")
                emit_ln(ARN, ARN, affine=True, add_to=R)

            sCC = "sT"
            for s in range(nsc):
                w.wait(Y, "sV", w.c["sV"])
                wait_all_st(Y)
                lv = load(sb3(R[:], TC), fmview(xT_own, 0, D, s * TC, TC))
                w.wait_t(V, lv)
                w.wait_t(S, lv)
                w.wait_t(T, lv)
                w.bar()
                PSg = emit_lnq(R)
                emit_ffn(PSg)
                if deferred_convs:
                    deferred_convs.pop(0)()
                if stage == "ffn1":
                    stv = store(fmview(yT, 0, D, s * TC, TC), sb3(R[:], TC))
                    continue
                store(fmview(sx1, 0, D, s * TC, TC), sb3(R[:], TC))
                emit_ln(R, Y2F)
                w.wait(S, "sV", w.c["sV"])
                for c in range(KC):
                    sop(lambda s_, c=c: s_.activation(
                        Y2B[:, c * TC:(c + 1) * TC], Y2F[:, c * TC:(c + 1) * TC],
                        AF.Identity))
                store(fmview(sy2, 0, D, s * TC, TC), sb3(Y2F[:, 0:KC * TC], TC))
                store(fmview(sy2b, 0, D, s * TC, TC), sb3(Y2B[:], TC),
                      after=("sS", w.c["sS"]))
                if s == 0:
                    wait_all_st(Y)
                PSp = PieceStream(precompute_srcs())
                emit_zab()
                store(sabf[0:128, s * NT * 32:(s + 1) * NT * 32], ABF[:],
                      after=("sS", w.c["sS"]))
                emit_precompute(PSp)
                for (buf, scr) in ((RVB, srv), (RQB, srq), (MVB, smv), (MQB, smq)):
                    store(fmview(scr, 0, D, s * TC, TC), sb3(buf[:], TC),
                          after=("sS", w.c["sS"]))
                for j in range(NT):
                    emit_scan_chunk(j, mode="agg")
                stv = store(sloc[0:128, s * NT * 192:(s + 1) * NT * 192], SLOC[:],
                            after=("sS", w.c["sS"]))
                w.bar()
            if stage == "ffn1":
                wait_all_st(Gq)
                wait_all_st(Y)
                return

            wait_all_st(V)
            w.wait(V, "sS", w.c["sS"])
            vopd(lambda v: v.tensor_copy(SLOC[:, 0:64], SRF[:]))
            vop(lambda v: v.tensor_copy(SLOC[:, 64:192], SMF[:]))
            ccst = store(ccin.rearrange("o (p n) -> (o p) n", p=128), SLOC[:, 0:192])
            w.wait_t(Gq, ccst)
            i = w.do(Gq, lambda g_: g_.collective_compute(
                "AllGather", mybir.AluOpType.bypass,
                replica_groups=[list(range(n_replicas))],
                ins=[ccin], outs=[ccout]))
            ccv = w.inc("sCCs", i, 1)
            w.wait(Y, "sCCs", ccv)
            lvg = load(GATH[:, 0:n_replicas * 192].rearrange("p (r n) -> p r n", n=192),
                       ccout.rearrange("r (p n) -> p r n", p=128))
            w.wait_t(V, lvg)
            lvs = load(SEL[:], sel_in)
            w.wait_t(V, lvs)
            for r in range(n_replicas):
                if r == 0:
                    vopd(lambda v: v.tensor_scalar(
                        SINI[:], GATH[:, 0:192], SEL[:, 0:1], None, ALU.mult))
                else:
                    vopd(lambda v, r=r: v.scalar_tensor_tensor(
                        SQ2[:, 0:192], GATH[:, r * 192:(r + 1) * 192], SEL[:, r:r + 1],
                        SINI[:], ALU.mult, ALU.add))
                    vopd(lambda v: v.tensor_copy(SINI[:], SQ2[:, 0:192]))
            w.bar()

            for s in range(nsc):
                w.wait(S, "sV", w.c["sV"])
                w.wait(S, "sS", w.c["sS"])
                wait_all_st(S)
                loads = []
                loads.append(loadS(sb3(R[:], TC), fmview(sx1, 0, D, s * TC, TC)))
                loads.append(loadS(sb3(Y2F[:, 0:KC * TC], TC), fmview(sy2, 0, D, s * TC, TC)))
                loads.append(loadS(sb3(Y2B[:], TC), fmview(sy2b, 0, D, s * TC, TC)))
                loads.append(loadS(ABF[:], sabf[0:128, s * NT * 32:(s + 1) * NT * 32]))
                for (buf, scr) in ((RVB, srv), (RQB, srq), (MVB, smv), (MQB, smq)):
                    loads.append(loadS(sb3(buf[:], TC), fmview(scr, 0, D, s * TC, TC)))
                if s == 0:
                    if NT > 1:
                        loads.append(loadS(SLOC[:, 192:NT * 192],
                                           sloc[0:128, 0:(NT - 1) * 192]))
                else:
                    loads.append(loadS(SLOC[:],
                                       sloc[0:128, (s * NT - 1) * 192:(s * NT + NT - 1) * 192]))
                for lv2 in loads:
                    w.wait_t(V, lv2)
                    w.wait_t(T, lv2)
                    w.wait_t(S, lv2)
                w.bar()
                for j in range(NT):
                    emit_scan_chunk(j, mode="use", g=s * NT + j)
                emit_epilogue()
                if stage == "mix":
                    stv = store(fmview(yT, 0, D, s * TC, TC), sb3(R[:], TC))
                    continue
                PSg = emit_lnq(R)
                emit_ffn(PSg)
                stv = store(fmview(yT, 0, D, s * TC, TC), sb3(R[:], TC))
            wait_all_st(Gq)
            wait_all_st(Y)

        block.sync(lambda eng: schedule(Wk("y", eng, sems)))
        block.tensor(lambda eng: schedule(Wk("t", eng, sems)))
        block.vector(lambda eng: schedule(Wk("v", eng, sems)))
        block.scalar(lambda eng: schedule(Wk("s", eng, sems)))
        block.gpsimd(lambda eng: schedule(Wk("g", eng, sems)))

    return nc



def _consts():
    s = np.arange(128)
    t = np.arange(128)
    m_ul = (s[:, None] <= t[None, :]).astype(np.float32)
    m_dc = (DECAY ** (t[None, :] - s[:, None])).astype(np.float32) * m_ul
    m_de = (DECAY ** (-(s[:, None] + 1.0))).astype(np.float32) * m_ul
    lam = (DECAY ** (np.arange(128) + 1.0)).astype(np.float32).reshape(128, 1)
    ddv = (DECAY ** (127.0 - np.arange(128))).astype(np.float32).reshape(128, 1)
    sg = np.arange(HALF).reshape(32, 128).T
    dvf = (DECAY ** (4095.0 - sg)).astype(np.float32)
    return m_ul, m_dc, m_de, lam, ddv, dvf


def _pp(b):
    return np.ascontiguousarray(b.reshape(KC, 128).T.astype(np.float32))


def kernel(**inputs):
    x = np.asarray(inputs["L0_attn_u"], np.float32)
    gw = {k: np.asarray(v, np.float32) for k, v in inputs.items()}
    m_ul, m_dc, m_de, lam, ddv, dvf = _consts()

    wz = np.zeros((32, D), np.float32)
    wz[0:8] = gw["L0_attn_reg_addr_w"]
    wz[8:24] = gw["L0_attn_mem_addr_w"]
    wz[24] = gw["L0_attn_reg_gate_w"][0]
    wz[25] = gw["L0_attn_mem_gate_w"][0]
    zb = np.zeros((1, 32), np.float32)
    zb[0, 0:8] = gw["L0_attn_reg_addr_b"]
    zb[0, 8:24] = gw["L0_attn_mem_addr_b"]
    zb[0, 24] = gw["L0_attn_reg_gate_b"][0]
    zb[0, 25] = gw["L0_attn_mem_gate_b"][0]

    comb = gw["L0_attn_comb_w"]
    shared = dict(
        wgT=np.ascontiguousarray(gw["L0_ffn_w_gate"].T),
        wuT=np.ascontiguousarray(gw["L0_ffn_w_up"].T),
        wdT=np.ascontiguousarray(gw["L0_ffn_w_down"].T),
        wvrT=np.ascontiguousarray(gw["L0_attn_reg_val_w"].T),
        wqrT=np.ascontiguousarray(gw["L0_attn_reg_q_w"].T),
        wvmT=np.ascontiguousarray(gw["L0_attn_mem_val_w"].T),
        wqmT=np.ascontiguousarray(gw["L0_attn_mem_q_w"].T),
        combT=np.ascontiguousarray(comb.T),
        wzT=np.ascontiguousarray(wz.T),
        zbrow=zb,
        bvr_pp=_pp(gw["L0_attn_reg_val_b"]),
        bqr_pp=_pp(gw["L0_attn_reg_q_b"]),
        bvm_pp=_pp(gw["L0_attn_mem_val_b"]),
        bqm_pp=_pp(gw["L0_attn_mem_q_b"]),
        combb_pp=_pp(gw["L0_attn_comb_b"]),
        lng_pp=_pp(gw["L0_attn_ln_g"]),
        lnb_pp=_pp(gw["L0_attn_ln_b"]),
        bvr_row=gw["L0_attn_reg_val_b"].reshape(1, D),
        bvm_row=gw["L0_attn_mem_val_b"].reshape(1, D),
        maskUL=m_ul, maskDec=m_dc, maskDecE=m_de,
        lam_in=lam, ddv_in=ddv,
        ident_in=np.eye(128).astype(ml_dtypes.bfloat16),
        identf_in=np.eye(128, dtype=np.float32),
        ones_in=np.ones((128, 1), np.float32),
        onesbf_in=np.ones((128, 1), ml_dtypes.bfloat16),
        bc1_in=np.ones((1, 128), np.float32),
    )

    nc = build_program()
    in_maps = []
    for c in range(N_CORES):
        beta, h = c // 2, c % 2
        xo = np.ascontiguousarray(x[beta, h * HALF:(h + 1) * HALF, :].T)
        sel = np.zeros((128, 8), np.float32)
        if h == 1:
            sel[:, c - 1] = 1.0
        m = dict(shared)
        m["xT_own"] = xo
        m["sel_in"] = sel
        in_maps.append(m)

    res = run_bass_kernel_spmd(nc, in_maps, core_ids=list(range(N_CORES)))
    out = np.empty((B, L, D), np.float32)
    for c in range(N_CORES):
        beta, h = c // 2, c % 2
        out[beta, h * HALF:(h + 1) * HALF, :] = res.results[c]["yT"].T
    return out


# version 19
# speedup vs baseline: 1.0267x; 1.0267x over previous
import contextlib
import numpy as np
import ml_dtypes
import concourse.bass as bass
import concourse.mybir as mybir
from concourse.bass_utils import run_bass_kernel_spmd

F32 = mybir.dt.float32
BF16 = mybir.dt.bfloat16
I32 = mybir.dt.int32
RMAGIC = 12582912.0
AF = mybir.ActivationFunctionType
ALU = mybir.AluOpType
AX = mybir.AxisListType

B, L, D = 4, 8192, 1024
INNER = 4096
HALF = 4096
TC = 512
NSC = HALF // TC
TCH = 128
NT = TC // TCH
KC = D // 128
ICN = INNER // 128
NREG, NMEM = 8, 16
DECAY, SHARP = 0.995, 5.0
DEC128 = float(DECAY ** 128)
EPS = 1e-5
N_CORES = 8

NLD, NST, NLS = 8, 4, 4
LD_SEMS = tuple(f"sLD{i}" for i in range(NLD))
ST_SEMS = tuple(f"sST{i}" for i in range(NST))
LS_SEMS = tuple(f"sLS{i}" for i in range(NLS))
SEMS = LD_SEMS + ST_SEMS + LS_SEMS + ("sT", "sV", "sS", "sCCs")



def MMSG(t, out, lhsT, rhs, start=True, stop=True):
    return t.matmul(out, lhsT, rhs, start=start, stop=stop, skip_group_check=True)

class Wk:

    def __init__(self, E, eng, sems):
        self.E, self.eng, self.sems = E, eng, sems
        self.c = {k: 0 for k in SEMS}
        self.notes = {}
        self.psum_cons = {}
        self.bank_started = {}
        self.wb_lastuse = [0, 0, 0, 0]
        self.ld_rot = 0
        self.st_rot = 0
        self.ls_rot = 0

    def do(self, e, fn):
        return fn(self.eng) if self.E == e else None

    def inc(self, sem, instr, n=1):
        self.c[sem] += n
        if instr is not None:
            instr.then_inc(self.sems[sem], n)
        return self.c[sem]

    def wait(self, e, sem, val):
        if self.E == e and val > 0:
            self.eng.wait_ge(self.sems[sem], val)

    def wait_t(self, e, tok):
        self.wait(e, tok[0], tok[1])

    def bar(self, engines=("t", "v", "s")):
        if self.E == "v":
            self.eng.drain()
        if self.E == "s":
            self.eng.drain()
        snap = {"t": self.c["sT"], "v": self.c["sV"], "s": self.c["sS"]}
        sem_of = {"t": "sT", "v": "sV", "s": "sS"}
        for e in engines:
            if self.E == e:
                for o in engines:
                    if o != e and snap[o] > 0:
                        self.eng.wait_ge(self.sems[sem_of[o]], snap[o])

    def bank_open(self, bank):
        if bank in self.psum_cons:
            sem, cnt = self.psum_cons[bank]
            self.wait("t", sem, cnt)
        self.bank_started[bank] = False

    def bank_start(self, bank):
        if not self.bank_started.get(bank, False):
            self.bank_started[bank] = True
            return True
        return False

    def bank_consumed(self, bank, sem="sV"):
        self.psum_cons[bank] = (sem, self.c[sem])


def build_program(stage="full", nsc=NSC, n_replicas=8):
    nc = bass.Bass("TRN2", target_bir_lowering=False, debug=False)
    inp = {}

    def din(name, shape, dt=F32):
        inp[name] = nc.dram_tensor(name, shape, dt, kind="ExternalInput").ap()
        return inp[name]

    xT_own = din("xT_own", [D, HALF])
    wgT = din("wgT", [D, INNER])
    wuT = din("wuT", [D, INNER])
    wdT = din("wdT", [INNER, D])
    wvrT = din("wvrT", [D, D])
    wqrT = din("wqrT", [D, D])
    wvmT = din("wvmT", [D, D])
    wqmT = din("wqmT", [D, D])
    combT = din("combT", [3 * D, D])
    wzT = din("wzT", [D, 32])
    zbrow = din("zbrow", [1, 32])
    bvr_pp = din("bvr_pp", [128, KC])
    bqr_pp = din("bqr_pp", [128, KC])
    bvm_pp = din("bvm_pp", [128, KC])
    bqm_pp = din("bqm_pp", [128, KC])
    combb_pp = din("combb_pp", [128, KC])
    lng_pp = din("lng_pp", [128, KC])
    lnb_pp = din("lnb_pp", [128, KC])
    bvr_row = din("bvr_row", [1, D])
    bvm_row = din("bvm_row", [1, D])
    maskUL = din("maskUL", [128, 128])
    maskDec = din("maskDec", [128, 128])
    maskDecE = din("maskDecE", [128, 128])
    lam_in = din("lam_in", [128, 1])
    ddv_in = din("ddv_in", [128, 1])
    ident_in = din("ident_in", [128, 128], BF16)
    identf_in = din("identf_in", [128, 128])
    ones_in = din("ones_in", [128, 1])
    onesbf_in = din("onesbf_in", [128, 1], BF16)
    bc1_in = din("bc1_in", [1, 128])
    sel_in = din("sel_in", [128, 8])

    yT = nc.dram_tensor("yT", [D, HALF], F32, kind="ExternalOutput").ap()

    sx1 = nc.dram_tensor("sx1", [D, HALF], F32, kind="Internal").ap()
    sy2 = nc.dram_tensor("sy2", [D, HALF], F32, kind="Internal").ap()
    sy2b = nc.dram_tensor("sy2b", [D, HALF], BF16, kind="Internal").ap()
    srv = nc.dram_tensor("srv", [D, HALF], BF16, kind="Internal").ap()
    srq = nc.dram_tensor("srq", [D, HALF], BF16, kind="Internal").ap()
    smv = nc.dram_tensor("smv", [D, HALF], BF16, kind="Internal").ap()
    smq = nc.dram_tensor("smq", [D, HALF], BF16, kind="Internal").ap()
    sabf = nc.dram_tensor("sabf", [128, NSC * NT * 32], BF16, kind="Internal").ap()
    sloc = nc.dram_tensor("sloc", [128, NSC * NT * 192], F32, kind="Internal").ap()
    ccin = nc.dram_tensor("ccin", [1, 128 * 192], F32, kind="Internal").ap()
    ccout = nc.dram_tensor("ccout", [n_replicas, 128 * 192], F32, kind="Internal").ap()
    wgq = nc.dram_tensor("wgq", [D, INNER], BF16, kind="Internal").ap()
    wuq = nc.dram_tensor("wuq", [D, INNER], BF16, kind="Internal").ap()
    wdq = nc.dram_tensor("wdq", [INNER, D], BF16, kind="Internal").ap()
    wvrq = nc.dram_tensor("wvrq", [D, D], BF16, kind="Internal").ap()
    wqrq = nc.dram_tensor("wqrq", [D, D], BF16, kind="Internal").ap()
    wvmq = nc.dram_tensor("wvmq", [D, D], BF16, kind="Internal").ap()
    wqmq = nc.dram_tensor("wqmq", [D, D], BF16, kind="Internal").ap()
    combq = nc.dram_tensor("combq", [3 * D, D], BF16, kind="Internal").ap()

    es = contextlib.ExitStack()
    with es:
        sb = lambda n, sh, dt=F32: es.enter_context(nc.sbuf_tensor(n, sh, dt))
        psal = lambda n, sh, dt=F32: es.enter_context(nc.psum_tensor(n, sh, dt))

        R = sb("R", [128, KC * TC])
        ARN = sb("ARN", [128, KC * TC])
        MIDF = sb("MIDF", [128, 24 * 512])
        Y2B = sb("Y2B", [128, KC * TC], BF16)
        K1 = sb("K1", [128, KC * TC], BF16)
        RVB = sb("RVB", [128, KC * TC], BF16)
        RQB = sb("RQB", [128, KC * TC], BF16)
        MVB = sb("MVB", [128, KC * TC], BF16)
        MQB = sb("MQB", [128, KC * TC], BF16)
        VrTM = sb("VrTM", [128, D], BF16)
        VmTM = sb("VmTM", [128, D], BF16)
        QSCW = sb("QSCW", [128, 4 * 512])
        WB = [sb(f"WB{i}", [128, 4096], BF16) for i in range(3)]
        WZ = sb("WZ", [128, KC * 32], BF16)
        ZBB = sb("ZBB", [128, 32])
        GT8 = sb("GT8", [128, 2 * NT])
        BVr = sb("BVr", [128, KC])
        BQr = sb("BQr", [128, KC])
        BVm = sb("BVm", [128, KC])
        BQm = sb("BQm", [128, KC])
        CBB = sb("CBB", [128, KC])
        LNG = sb("LNG", [128, KC])
        LNB = sb("LNB", [128, KC])
        MKUL = sb("MKUL", [128, 128])
        MKDC = sb("MKDC", [128, 128])
        MKDE = sb("MKDE", [128, 128])
        LAM = sb("LAM", [128, 1])
        DDV = sb("DDV", [128, 1])
        IDN = sb("IDN", [128, 128], BF16)
        IDF = sb("IDF", [128, 128])
        ONE = sb("ONE", [128, 1])
        ONEBF = sb("ONEBF", [128, 1], BF16)
        BC1 = sb("BC1", [1, 128])
        CON = sb("CON", [128, 8])
        SRF = sb("SRF", [128, KC * NREG])
        SMF = sb("SMF", [128, KC * NMEM])
        SRFB = sb("SRFB", [128, KC * NREG], BF16)
        SMFB = sb("SMFB", [128, KC * NMEM], BF16)
        SRNB = sb("SRNB", [NREG, D], BF16)
        SMNB = sb("SMNB", [NMEM, D], BF16)
        ABT = sb("ABT", [128, NT * 32])
        ABF = sb("ABF", [128, NT * 32], BF16)
        BDCB = sb("BDCB", [128, NMEM], BF16)
        P1B = sb("P1B", [128, 256], BF16)
        PR = sb("PR", [128, NREG])
        PM = sb("PM", [128, NMEM])
        PRB = sb("PRB", [128, NREG], BF16)
        PMB = sb("PMB", [128, NMEM], BF16)
        SCM = sb("SCM", [128, NMEM])
        SMX = sb("SMX", [128, 24])
        TRS = sb("TRS", [16, 512], BF16)
        EMB = sb("EMB", [128, 256], BF16)
        ROW = [sb(f"ROW{i}", [1, TC]) for i in range(5)]
        ROW.append(ROW[0])
        RD = sb("RD", [1, D])
        RDB = sb("RDB", [1, D], BF16)
        MW = sb("MW", [1, 8])
        SAB = sb("SAB", [128, 1])
        BRD = sb("BRD", [128, TC])
        BRDG = sb("BRDG", [128, TC])
        SINI = sb("SINI", [128, 192])
        SEL = sb("SEL", [128, 8])
        SQ2 = sb("SQ2", [128, TC])
        SQ2B = sb("SQ2B", [128, TC])

        MX1 = sb("MX1", [128, TC])
        SMAG = MX1
        MXT = sb("MXT", [128, 2 * NT])

        Y2F = MIDF
        GATH = QSCW
        SLOC = QSCW[:, 1280:1280 + NT * 192]
        SNM = QSCW
        GBF = VrTM
        GFB = VmTM
        RRD = RVB
        MRD = MVB

        pb = [psal(f"pb{i}", [128, 512]) for i in range(6)]
        pstat = psal("pstat", [128, 512])
        ptr = psal("ptr", [128, 1024], BF16)

        sems = {k: es.enter_context(nc.semaphore(k)) for k in SEMS}
        block = es.enter_context(nc.Block())

        def schedule(w: Wk):
            T, V, S, Y, Gq = "t", "v", "s", "y", "g"

            def load(dst_ap, src_ap):
                k = LD_SEMS[w.ld_rot % NLD]
                w.ld_rot += 1
                w.wait(Y, k, w.c[k])
                i = w.do(Y, lambda y: y.dma_start(out=dst_ap, in_=src_ap))
                return (k, w.inc(k, i, 16))

            def loadS(dst_ap, src_ap):
                k = LS_SEMS[w.ls_rot % NLS]
                w.ls_rot += 1
                w.wait(S, k, w.c[k])
                i = w.do(S, lambda s_: s_.dma_start(out=dst_ap, in_=src_ap))
                return (k, w.inc(k, i, 16))

            def store(dst_ap, src_ap, after=None):
                if after is None:
                    after = ("sV", w.c["sV"])
                k = ST_SEMS[w.st_rot % NST]
                w.st_rot += 1
                w.wait(Gq, k, w.c[k])
                w.wait_t(Gq, after)
                i = w.do(Gq, lambda g: g.dma_start(out=dst_ap, in_=src_ap))
                return (k, w.inc(k, i, 16))

            def wait_all_st(e):
                for k in ST_SEMS:
                    w.wait(e, k, w.c[k])

            def fmview(dram, r0, nrow, c0, ncol):
                return dram[r0:r0 + nrow, c0:c0 + ncol].rearrange(
                    "(c p) n -> p c n", p=128)

            def sb3(sb_ap, ncol):
                return sb_ap.rearrange("p (c n) -> p c n", n=ncol)

            def vop(fn):
                return w.inc("sV", w.do(V, fn))

            def vopd(fn):
                w.do(V, lambda v: v.drain())
                return w.inc("sV", w.do(V, fn))

            def sop(fn):
                return w.inc("sS", w.do(S, fn))

            def sopd(fn):
                w.do(S, lambda s: s.drain())
                return w.inc("sS", w.do(S, fn))

            def top(fn):
                return w.inc("sT", w.do(T, fn))

            load(MKUL[:], maskUL)
            load(MKDC[:], maskDec)
            load(MKDE[:], maskDecE)
            load(LAM[:], lam_in)
            load(DDV[:], ddv_in)
            load(IDN[:], ident_in)
            load(IDF[:], identf_in)
            load(ONE[:], ones_in)
            load(ONEBF[:], onesbf_in)
            load(BC1[:], bc1_in)
            load(BVr[:], bvr_pp)
            load(BQr[:], bqr_pp)
            load(BVm[:], bvm_pp)
            load(BQm[:], bqm_pp)
            load(CBB[:], combb_pp)
            load(LNG[:], lng_pp)
            load(LNB[:], lnb_pp)
            for k in LD_SEMS:
                w.wait(V, k, w.c[k])
            vop(lambda v: v.memset(CON[:, 0:1], 0.03125))
            vop(lambda v: v.memset(CON[:, 1:2], -SHARP))
            vop(lambda v: v.memset(CON[:, 2:3], -0.5))
            vop(lambda v: v.memset(CON[:, 3:4], 1.0))
            vop(lambda v: v.memset(CON[:, 4:5], RMAGIC))
            vop(lambda v: v.memset(CON[:, 5:6], -RMAGIC))
            vop(lambda v: v.memset(ABF[:], 0.0))
            vop(lambda v: v.memset(SRF[:], 0.0))
            vop(lambda v: v.memset(SMF[:], 0.0))

            stg = [MIDF[:, i * 2048:(i + 1) * 2048] for i in range(4)]
            stgB = [MIDF[:, 8192 + i * 2048: 8192 + (i + 1) * 2048] for i in range(2)]
            stgB.append(QSCW[:, 0:2048])

            def prep_ffn(matT, scratch_q, nrow, ncol, m_slot):
                colw = min(2048, ncol)
                nch = (nrow // 128) * (ncol // colw)
                ncolc = ncol // colw
                w.wait(S, "sT", w.c["sT"])
                w.wait(Y, "sS", w.c["sS"])
                w.bank_open("pstat")
                for ci in range(nch):
                    r = (ci // ncolc) * 128
                    cc = (ci % ncolc) * colw
                    slot = stg[ci % 4]
                    if ci >= 4:
                        w.wait(Y, "sS", w.notes[("pA", ci - 4)])
                    lv = load(slot[:, 0:colw], matT[r:r + 128, cc:cc + colw])
                    w.wait_t(S, lv)
                    ab = ARN[:, (ci % 2) * 2048:(ci % 2) * 2048 + colw]
                    if ci >= 2:
                        w.wait(S, "sT", w.notes[("pAt", ci - 2)])
                    w.notes[("pA", ci)] = sop(
                        lambda s_, slot=slot, ab=ab: s_.activation(ab, slot[:, 0:colw], AF.Abs))
                    w.wait(T, "sS", w.c["sS"])
                    for qn in range(colw // 512):
                        top(lambda t, ab=ab, qn=qn, ci=ci: MMSG(
                            t, pstat[0:1, 0:512], ONE[:], ab[:, qn * 512:(qn + 1) * 512],
                            start=w.bank_start("pstat"),
                            stop=(ci == nch - 1 and qn == colw // 512 - 1)))
                    w.notes[("pAt", ci)] = w.c["sT"]
                w.wait(V, "sT", w.c["sT"])
                vopd(lambda v: v.tensor_reduce(ROW[0][0:1, 0:1], pstat[0:1, 0:512],
                                               AX.X, ALU.add))
                w.bank_consumed("pstat")
                vopd(lambda v: v.tensor_scalar(ROW[0][0:1, 0:1], ROW[0][0:1, 0:1],
                                              1.0 / (nrow * ncol), 1e-5, ALU.mult, ALU.max))
                vopd(lambda v, m=m_slot: v.tensor_copy(MW[0:1, m:m + 1], ROW[0][0:1, 0:1]))
                vop(lambda v: v.reciprocal(ROW[0][0:1, 1:2], ROW[0][0:1, 0:1]))
                w.wait(T, "sV", w.c["sV"])
                w.bank_open("pb4")
                top(lambda t: MMSG(t, pb[4][:, 0:1], BC1[:], ROW[0][0:1, 1:2],
                                       start=w.bank_start("pb4"), stop=True))
                w.wait(V, "sT", w.c["sT"])
                vop(lambda v: v.tensor_copy(SQ2[:, 1:2], pb[4][:, 0:1]))
                w.bank_consumed("pb4")
                w.wait(S, "sV", w.c["sV"])
                wait_all_st(V)
                for ci in range(nch):
                    r = (ci // ncolc) * 128
                    cc = (ci % ncolc) * colw
                    slot = stgB[ci % 3]
                    if ci >= 3:
                        w.wait(S, "sV", w.notes[("pBm", ci - 3)])
                    lvB = loadS(slot[:, 0:colw], matT[r:r + 128, cc:cc + colw])
                    par = (ci % 2) * 2048
                    w.wait_t(V, lvB)
                    w.notes[("pBm", ci)] = vop(lambda v, slot=slot, par=par: v.tensor_scalar(
                        R[:, par:par + colw], slot[:, 0:colw], SQ2[:, 1:2], RMAGIC,
                        ALU.mult, ALU.add))
                    vopd(lambda v, par=par: v.tensor_scalar(
                        R[:, par:par + colw], R[:, par:par + colw],
                        -RMAGIC, -1.0, ALU.add, ALU.max))
                    if ci >= 2:
                        w.wait_t(V, w.notes[("pB", ci - 2)])
                    last = vopd(lambda v, par=par: v.tensor_scalar(
                        WB[2][:, par:par + colw], R[:, par:par + colw],
                        1.0, None, ALU.min))
                    w.notes[("pB", ci)] = store(
                        scratch_q[r:r + 128, cc:cc + colw], WB[2][:, par:par + colw],
                        after=("sV", last))

            prep_ffn(wgT, wgq, D, INNER, 0)
            prep_ffn(wuT, wuq, D, INNER, 1)
            prep_ffn(wdT, wdq, INNER, D, 2)
            vopd(lambda v: v.tensor_tensor(MW[0:1, 3:4], MW[0:1, 1:2], MW[0:1, 2:3], ALU.mult))
            vopd(lambda v: v.tensor_scalar(MW[0:1, 3:4], MW[0:1, 3:4], 0.5 / (127.0 * 127.0), None, ALU.mult))
            vopd(lambda v: v.tensor_scalar(MW[0:1, 4:5], MW[0:1, 0:1], 1.0 / 127.0, None, ALU.mult))

            def conv_bf16(matT, scratch_q, nrow):
                w.wait(Y, "sV", w.c["sV"])
                w.wait(Y, "sS", w.c["sS"])
                wait_all_st(S)
                for ci in range(nrow // 128):
                    slot = stg[ci % 4]
                    q = ci % 4
                    if ci >= 4:
                        w.wait(Y, "sS", w.notes[("cBv", ci - 4)])
                    lv = load(slot[:, 0:1024], matT[ci * 128:(ci + 1) * 128, 0:1024])
                    w.wait_t(S, lv)
                    if ci >= 4:
                        w.wait_t(S, w.notes[("cBst", ci - 4)])
                    lasts = sop(lambda s_, slot=slot, q=q: s_.activation(
                        K1[:, q * 1024:(q + 1) * 1024], slot[:, 0:1024], AF.Identity))
                    w.notes[("cBv", ci)] = lasts
                    w.notes[("cBst", ci)] = store(
                        scratch_q[ci * 128:(ci + 1) * 128, 0:1024],
                        K1[:, q * 1024:(q + 1) * 1024], after=("sS", lasts))

            conv_bf16(wvrT, wvrq, D)
            conv_bf16(wqrT, wqrq, D)
            conv_bf16(wvmT, wvmq, D)
            conv_bf16(wqmT, wqmq, D)
            deferred_convs = [lambda: conv_bf16(combT, combq, 3 * D)]
            if stage == "ffn1":
                deferred_convs = []
            for c in range(KC):
                w.wait(Y, "sV", w.c["sV"])
                w.wait(Y, "sS", w.c["sS"])
                lv = load(stg[0][:, 0:32], wzT[c * 128:(c + 1) * 128, 0:32])
                w.wait_t(V, lv)
                vop(lambda v, c=c: v.tensor_copy(WZ[:, c * 32:(c + 1) * 32], stg[0][:, 0:32]))
            lv = load(ROW[0][0:1, 0:32], zbrow)
            w.wait_t(T, lv)
            w.wait(T, "sV", w.c["sV"])
            w.bank_open("pb4")
            top(lambda t: MMSG(t, pb[4][:, 0:32], BC1[:], ROW[0][0:1, 0:32],
                                   start=w.bank_start("pb4"), stop=True))
            w.wait(V, "sT", w.c["sT"])
            vop(lambda v: v.tensor_copy(ZBB[:], pb[4][:, 0:32]))
            w.bank_consumed("pb4")
            wait_all_st(Y)
            wait_all_st(V)
            w.bar()

            class PieceStream:
                def __init__(self, srcs):
                    self.srcs, self.rdy, self.nxt = srcs, {}, 0
                    for _ in range(min(3, len(srcs))):
                        self._pf()

                def _pf(self):
                    j = self.nxt
                    if j >= len(self.srcs):
                        return
                    view3, ncol = self.srcs[j]
                    w.wait(Y, "sT", w.wb_lastuse[j % 3])
                    self.rdy[j] = load(sb3(WB[j % 3][:], ncol), view3)
                    self.nxt += 1

                def begin(self, j):
                    w.wait_t(T, self.rdy[j])
                    return WB[j % 3]

                def end(self, j):
                    w.wb_lastuse[j % 3] = w.c["sT"]
                    self._pf()

            def bcast(row_ap, dst):
                w.wait(T, "sV", w.c["sV"])
                w.bank_open("pb4")
                top(lambda t: MMSG(t, pb[4][:, 0:TC], BC1[:], row_ap[0:1, 0:TC],
                                       start=w.bank_start("pb4"), stop=True))
                w.wait(S, "sT", w.c["sT"])
                sop(lambda s: s.activation(dst[:, 0:TC], pb[4][:, 0:TC], AF.Identity))
                w.bank_consumed("pb4", "sS")
                w.wait(V, "sS", w.c["sS"])

            def maxabs_rows(src, nch, row_out):
                sl = src if callable(src) else (lambda c, b=src: b[:, c * TC:(c + 1) * TC])
                w.wait(S, "sV", w.c["sV"])
                for c in range(nch):
                    qb = SQ2 if c % 2 == 0 else SQ2B
                    if c >= 2:
                        w.wait(S, "sV", w.notes[("mxq", c - 2)])
                    sop(lambda s, c=c, qb=qb: s.activation(qb[:], sl(c), AF.Abs))
                    w.wait(V, "sS", w.c["sS"])
                    if c == 0:
                        vopd(lambda v, qb=qb: v.tensor_copy(MX1[:], qb[:]))
                    else:
                        vopd(lambda v, qb=qb: v.tensor_tensor(MX1[:], MX1[:], qb[:], ALU.max))
                    w.notes[("mxq", c)] = w.c["sV"]
                w.wait(T, "sV", w.c["sV"])
                w.bank_open("pstat")
                for j in range(NT):
                    top(lambda t, j=j: t.transpose(
                        pstat[:, j * 128:(j + 1) * 128], MX1[:, j * 128:(j + 1) * 128], IDF[:]))
                    if j == 0:
                        w.bank_started["pstat"] = True
                w.wait(V, "sT", w.c["sT"])
                for j in range(NT):
                    vop(lambda v, j=j: v.tensor_reduce(
                        MXT[:, j:j + 1], pstat[:, j * 128:(j + 1) * 128], AX.X, ALU.max))
                w.bank_consumed("pstat")
                w.wait(T, "sV", w.c["sV"])
                w.bank_open("pstat")
                for j in range(NT):
                    top(lambda t, j=j: t.transpose(
                        pstat[0:1, j * 128:(j + 1) * 128], MXT[:, j:j + 1], IDF[:]))
                w.wait(V, "sT", w.c["sT"])
                vopd(lambda v: v.tensor_copy(row_out[0:1, 0:TC], pstat[0:1, 0:TC]))
                w.bank_consumed("pstat")

            def maxabs_finalize(row_out):
                w.wait(T, "sV", w.c["sV"])
                w.bank_open("pstat")
                for j in range(NT):
                    top(lambda t, j=j: t.transpose(
                        pstat[:, j * 128:(j + 1) * 128], MX1[:, j * 128:(j + 1) * 128], IDF[:]))
                    if j == 0:
                        w.bank_started["pstat"] = True
                w.wait(V, "sT", w.c["sT"])
                for j in range(NT):
                    vop(lambda v, j=j: v.tensor_reduce(
                        MXT[:, j:j + 1], pstat[:, j * 128:(j + 1) * 128], AX.X, ALU.max))
                w.bank_consumed("pstat")
                w.wait(T, "sV", w.c["sV"])
                w.bank_open("pstat")
                for j in range(NT):
                    top(lambda t, j=j: t.transpose(
                        pstat[0:1, j * 128:(j + 1) * 128], MXT[:, j:j + 1], IDF[:]))
                w.wait(V, "sT", w.c["sT"])
                vopd(lambda v: v.tensor_copy(row_out[0:1, 0:TC], pstat[0:1, 0:TC]))
                w.bank_consumed("pstat")

            def emit_ln(src, dst, affine=False, add_to=None):
                w.bar()
                for c in range(KC):
                    qb = SQ2 if c % 2 == 0 else SQ2B
                    sop(lambda s, c=c, qb=qb: s.activation(qb[:], src[:, c * TC:(c + 1) * TC], AF.Square))
                    w.wait(T, "sS", w.c["sS"])
                    if c == 0:
                        w.bank_open("pstat")
                    top(lambda t, c=c: MMSG(t,
                        pstat[0:1, 0:TC], ONE[:], src[:, c * TC:(c + 1) * TC],
                        start=w.bank_start("pstat"), stop=(c == KC - 1)))
                    top(lambda t, c=c, qb=qb: MMSG(t,
                        pstat[32:33, 0:TC], ONE[:], qb[:],
                        start=(c == 0), stop=(c == KC - 1)))
                    w.notes[("lnq", c)] = w.c["sT"]
                    if c >= 1:
                        w.wait(S, "sT", w.notes[("lnq", c - 1)])
                w.wait(V, "sT", w.c["sT"])
                vopd(lambda v: v.tensor_scalar(ROW[0][:], pstat[0:1, 0:TC], 1.0 / D, None, ALU.mult))
                vop(lambda v: v.tensor_scalar(ROW[1][:], pstat[32:33, 0:TC], 1.0 / D, None, ALU.mult))
                w.bank_consumed("pstat")
                vopd(lambda v: v.tensor_tensor(ROW[2][:], ROW[0][:], ROW[0][:], ALU.mult))
                vopd(lambda v: v.tensor_sub(ROW[1][:], ROW[1][:], ROW[2][:]))
                vopd(lambda v: v.tensor_scalar_add(ROW[1][:], ROW[1][:], EPS))
                w.wait(S, "sV", w.c["sV"])
                sop(lambda s: s.activation(ROW[2][:], ROW[1][:], AF.Ln))
                sopd(lambda s: s.activation(ROW[2][:], ROW[2][:], AF.Exp, scale=CON[0:1, 2:3]))
                w.wait(V, "sS", w.c["sS"])
                vopd(lambda v: v.scalar_tensor_tensor(ROW[3][:], ROW[0][:], -1.0, ROW[2][:],
                                                     ALU.mult, ALU.mult))
                w.wait(T, "sV", w.c["sV"])
                w.bank_open("pb4")
                top(lambda t: MMSG(t, pb[4][:, 0:TC], BC1[:], ROW[2][0:1, 0:TC],
                                   start=w.bank_start("pb4"), stop=True))
                w.bank_open("pb5")
                top(lambda t: MMSG(t, pb[5][:, 0:TC], BC1[:], ROW[3][0:1, 0:TC],
                                   start=w.bank_start("pb5"), stop=True))
                w.wait(V, "sT", w.c["sT"])
                for c in range(KC):
                    f = vopd if c == 0 else vop
                    f(lambda v, c=c: v.tensor_tensor(
                        dst[:, c * TC:(c + 1) * TC], src[:, c * TC:(c + 1) * TC], pb[4][:, 0:TC], ALU.mult))
                for c in range(KC):
                    f = vopd if c == 0 else vop
                    f(lambda v, c=c: v.tensor_tensor(
                        dst[:, c * TC:(c + 1) * TC], dst[:, c * TC:(c + 1) * TC], pb[5][:, 0:TC], ALU.add))
                w.bank_consumed("pb4")
                w.bank_consumed("pb5")
                if affine:
                    for c in range(KC):
                        f = vopd if c == 0 else vop
                        f(lambda v, c=c: v.tensor_scalar(
                            dst[:, c * TC:(c + 1) * TC], dst[:, c * TC:(c + 1) * TC],
                            LNG[:, c:c + 1], LNB[:, c:c + 1], ALU.mult, ALU.add))
                if add_to is not None:
                    for c in range(KC):
                        f = vopd if c == 0 else vop
                        f(lambda v, c=c: v.tensor_tensor(
                            add_to[:, c * TC:(c + 1) * TC], add_to[:, c * TC:(c + 1) * TC],
                            dst[:, c * TC:(c + 1) * TC], ALU.add))
                w.bar()

            def emit_lnq(src):
                w.bar()
                w.bank_open("pb4")
                for c in range(KC):
                    qb = SQ2 if c % 2 == 0 else SQ2B
                    sop(lambda s, c=c, qb=qb: s.activation(
                        qb[:], src[:, c * TC:(c + 1) * TC], AF.Square))
                    w.wait(T, "sS", w.c["sS"])
                    top(lambda t, c=c: MMSG(t,
                        pb[4][0:1, 0:TC], ONE[:], src[:, c * TC:(c + 1) * TC],
                        start=w.bank_start("pb4"), stop=(c == KC - 1)))
                    top(lambda t, c=c, qb=qb: MMSG(t,
                        pb[4][32:33, 0:TC], ONE[:], qb[:],
                        start=(c == 0), stop=(c == KC - 1)))
                    w.notes[("lnq", c)] = w.c["sT"]
                    if c >= 1:
                        w.wait(S, "sT", w.notes[("lnq", c - 1)])
                sl = lambda c: src[:, c * TC:(c + 1) * TC]
                qs = lambda q: QSCW[:, q * TC:(q + 1) * TC]
                for (acc, op) in ((MX1, ALU.max), (BRD, ALU.min)):
                    for q in range(4):
                        f = vopd if q == 0 else vop
                        f(lambda v, q=q, op=op: v.tensor_tensor(
                            qs(q), sl(2 * q), sl(2 * q + 1), op))
                    vopd(lambda v, op=op: v.tensor_tensor(qs(0), qs(0), qs(1), op))
                    vop(lambda v, op=op: v.tensor_tensor(qs(2), qs(2), qs(3), op))
                    vopd(lambda v, acc=acc, op=op: v.tensor_tensor(
                        acc[:, 0:TC], qs(0), qs(2), op))
                w.wait(T, "sV", w.c["sV"])
                w.bank_open("pstat")
                for j in range(NT):
                    top(lambda t, j=j: t.transpose(
                        pstat[:, j * 128:(j + 1) * 128], MX1[:, j * 128:(j + 1) * 128], IDF[:]))
                    if j == 0:
                        w.bank_started["pstat"] = True
                w.wait(V, "sT", w.c["sT"])
                for j in range(NT):
                    vop(lambda v, j=j: v.tensor_reduce(
                        MXT[:, 2 * j:2 * j + 1], pstat[:, j * 128:(j + 1) * 128], AX.X, ALU.max))
                w.bank_consumed("pstat")
                w.wait(T, "sV", w.c["sV"])
                w.bank_open("pstat")
                for j in range(NT):
                    top(lambda t, j=j: t.transpose(
                        pstat[:, j * 128:(j + 1) * 128], BRD[:, j * 128:(j + 1) * 128], IDF[:]))
                    if j == 0:
                        w.bank_started["pstat"] = True
                w.wait(V, "sT", w.c["sT"])
                for j in range(NT):
                    vop(lambda v, j=j: v.tensor_reduce(
                        MXT[:, 2 * j + 1:2 * j + 2], pstat[:, j * 128:(j + 1) * 128], AX.X, ALU.min))
                w.bank_consumed("pstat")
                w.wait(T, "sV", w.c["sV"])
                w.bank_open("pstat")
                w.bank_open("pb5")
                for j in range(NT):
                    top(lambda t, j=j: t.transpose(
                        pstat[0:1, j * 128:(j + 1) * 128], MXT[:, 2 * j:2 * j + 1], IDF[:]))
                    if j == 0:
                        w.bank_started["pstat"] = True
                        w.bank_started["pb5"] = True
                    top(lambda t, j=j: t.transpose(
                        pb[5][0:1, j * 128:(j + 1) * 128], MXT[:, 2 * j + 1:2 * j + 2], IDF[:]))
                w.wait(V, "sT", w.c["sT"])
                vopd(lambda v: v.tensor_copy(ROW[4][:], pstat[0:1, 0:TC]))
                vop(lambda v: v.tensor_copy(ROW[3][:], pb[5][0:1, 0:TC]))
                w.bank_consumed("pstat")
                w.bank_consumed("pb5")
                vopd(lambda v: v.tensor_scalar(ROW[0][:], pb[4][0:1, 0:TC], 1.0 / D, None, ALU.mult))
                vop(lambda v: v.tensor_scalar(ROW[1][:], pb[4][32:33, 0:TC], 1.0 / D, None, ALU.mult))
                w.bank_consumed("pb4")
                vopd(lambda v: v.tensor_tensor(ROW[2][:], ROW[0][:], ROW[0][:], ALU.mult))
                vopd(lambda v: v.tensor_sub(ROW[1][:], ROW[1][:], ROW[2][:]))
                vopd(lambda v: v.tensor_scalar_add(ROW[1][:], ROW[1][:], EPS))
                w.wait(S, "sV", w.c["sV"])
                sop(lambda s: s.activation(ROW[2][:], ROW[1][:], AF.Ln))
                sopd(lambda s: s.activation(ROW[2][:], ROW[2][:], AF.Exp, scale=CON[0:1, 2:3]))
                vopd(lambda v: v.tensor_sub(ROW[4][:], ROW[4][:], ROW[0][:]))
                vop(lambda v: v.tensor_sub(ROW[3][:], ROW[0][:], ROW[3][:]))
                vopd(lambda v: v.tensor_tensor(ROW[4][:], ROW[4][:], ROW[3][:], ALU.max))
                w.wait(V, "sS", w.c["sS"])
                vopd(lambda v: v.tensor_tensor(ROW[4][:], ROW[4][:], ROW[2][:], ALU.mult))
                vopd(lambda v: v.tensor_scalar(ROW[4][:], ROW[4][:], 1e-5, None, ALU.max))
                vopd(lambda v: v.reciprocal(ROW[1][:], ROW[4][:]))
                vopd(lambda v: v.tensor_scalar(ROW[1][:], ROW[1][:], 127.0, None, ALU.mult))
                vopd(lambda v: v.tensor_tensor(ROW[2][:], ROW[2][:], ROW[1][:], ALU.mult))
                vopd(lambda v: v.scalar_tensor_tensor(ROW[3][:], ROW[0][:], -1.0, ROW[2][:],
                                                      ALU.mult, ALU.mult))
                bcast(ROW[2], BRD)
                bcast(ROW[3], BRDG)
                srcs = []
                for p in range(8):
                    srcs.append((fmview(wgq, 0, D, p * 512, 512), 512))
                    srcs.append((fmview(wuq, 0, D, p * 512, 512), 512))
                PS = PieceStream(srcs)
                wait_all_st(V)
                for g4 in range(KC // 4):
                    for q in range(4):
                        c = g4 * 4 + q
                        f = vopd if q == 0 else vop
                        f(lambda v, c=c, q=q: v.tensor_tensor(
                            QSCW[:, q * TC:(q + 1) * TC], src[:, c * TC:(c + 1) * TC],
                            BRD[:], ALU.mult))
                    for q in range(4):
                        f = vopd if q == 0 else vop
                        f(lambda v, q=q: v.tensor_tensor(
                            QSCW[:, q * TC:(q + 1) * TC], QSCW[:, q * TC:(q + 1) * TC],
                            BRDG[:], ALU.add))
                    for q in range(4):
                        c = g4 * 4 + q
                        f = vopd if q == 0 else vop
                        f(lambda v, c=c, q=q: v.tensor_scalar(
                            K1[:, c * TC:(c + 1) * TC], QSCW[:, q * TC:(q + 1) * TC],
                            RMAGIC, -RMAGIC, ALU.add, ALU.add))
                return PS

            K2 = [RVB, RQB, MVB, MQB]

            def k2sl(ic):
                return K2[ic // 8][:, (ic % 8) * TC:(ic % 8 + 1) * TC]

            def mid_sl(ic):
                if ic < KC:
                    return ARN[:, ic * TC:(ic + 1) * TC]
                return MIDF[:, (ic - KC) * TC:(ic - KC + 1) * TC]

            def emit_ffn(PS):
                wait_all_st(S)
                wait_all_st(V)
                vopd(lambda v: v.tensor_scalar(ROW[3][:], ROW[4][:], MW[0:1, 4:5], None, ALU.mult))
                bcast(ROW[3], BRDG)
                w.bar()
                for p in range(8):
                    bufg = PS.begin(2 * p)
                    bufu = None
                    for icl in range(4):
                        ic = p * 4 + icl
                        rg3 = ic % 3
                        bg, bu = pb[rg3], pb[3 + rg3]
                        w.bank_open(f"pb{rg3}")
                        for c in range(KC):
                            top(lambda t, c=c, icl=icl, bufg=bufg, bg=bg: MMSG(t,
                                bg[:, 0:TC],
                                bufg[:, c * 512 + icl * 128: c * 512 + (icl + 1) * 128],
                                K1[:, c * TC:(c + 1) * TC],
                                start=w.bank_start(f"pb{rg3}") if c == 0 else False,
                                stop=(c == KC - 1)))
                        if icl == 0:
                            bufu = PS.begin(2 * p + 1)
                        w.bank_open(f"pb{3 + rg3}")
                        for c in range(KC):
                            top(lambda t, c=c, icl=icl, bufu=bufu, bu=bu: MMSG(t, 
                                bu[:, 0:TC],
                                bufu[:, c * 512 + icl * 128: c * 512 + (icl + 1) * 128],
                                K1[:, c * TC:(c + 1) * TC],
                                start=w.bank_start(f"pb{3 + rg3}") if c == 0 else False,
                                stop=(c == KC - 1)))
                        w.wait(V, "sT", w.c["sT"])
                        vopd(lambda v, bg=bg: v.tensor_tensor(SQ2[:], bg[:, 0:TC], BRDG[:], ALU.mult))
                        w.bank_consumed(f"pb{rg3}")
                        w.wait(S, "sV", w.c["sV"])
                        sop(lambda s: s.activation(SQ2[:], SQ2[:], AF.Silu))
                        w.wait(V, "sS", w.c["sS"])
                        vop(lambda v, ic=ic, bu=bu: v.tensor_tensor(
                            mid_sl(ic), SQ2[:], bu[:, 0:TC], ALU.mult))
                        w.bank_consumed(f"pb{3 + rg3}")
                        qb = SQ2B if ic % 2 == 0 else SQ2
                        w.wait(S, "sV", w.c["sV"])
                        sop(lambda s, ic=ic, qb=qb: s.activation(qb[:], mid_sl(ic), AF.Abs))
                        w.wait(V, "sS", w.c["sS"])
                        if ic == 0:
                            vopd(lambda v, qb=qb: v.tensor_copy(MX1[:], qb[:]))
                        else:
                            vopd(lambda v, qb=qb: v.tensor_tensor(MX1[:], MX1[:], qb[:], ALU.max))
                    PS.end(2 * p)
                    PS.end(2 * p + 1)
                srcsd = [(fmview(wdq, p * 512, 512, 0, 1024), 1024) for p in range(8)]
                PSd = PieceStream(srcsd + srcsd)
                maxabs_finalize(ROW[2])
                vopd(lambda v: v.tensor_scalar(ROW[2][:], ROW[2][:], 1e-5, None, ALU.max))
                vopd(lambda v: v.reciprocal(ROW[1][:], ROW[2][:]))
                vopd(lambda v: v.tensor_scalar(ROW[1][:], ROW[1][:], 127.0, None, ALU.mult))
                bcast(ROW[1], BRD)
                vopd(lambda v: v.tensor_tensor(ROW[3][:], ROW[2][:], ROW[4][:], ALU.mult))
                vopd(lambda v: v.tensor_scalar(ROW[3][:], ROW[3][:], MW[0:1, 3:4], None, ALU.mult))
                bcast(ROW[3], BRDG)
                w.bar()
                for h in range(2):
                    for dl in range(4):
                        w.bank_open(f"pb{dl}")
                    for p in range(8):
                        if h == 0:
                            for q in range(4):
                                ic = p * 4 + q
                                f = vopd if q == 0 else vop
                                f(lambda v, ic=ic, q=q: v.tensor_tensor(
                                    QSCW[:, q * TC:(q + 1) * TC], mid_sl(ic), BRD[:], ALU.mult))
                            for q in range(4):
                                ic = p * 4 + q
                                f = vopd if q == 0 else vop
                                f(lambda v, ic=ic, q=q: v.tensor_scalar(
                                    k2sl(ic), QSCW[:, q * TC:(q + 1) * TC],
                                    RMAGIC, -RMAGIC, ALU.add, ALU.add))
                            w.wait(T, "sV", w.c["sV"])
                        buf = PSd.begin(h * 8 + p)
                        for kcl in range(4):
                            ic = p * 4 + kcl
                            for dl in range(4):
                                dout = h * 4 + dl
                                top(lambda t, kcl=kcl, dout=dout, dl=dl, buf=buf, ic=ic: MMSG(t, 
                                    pb[dl][:, 0:TC],
                                    buf[:, kcl * 1024 + dout * 128: kcl * 1024 + (dout + 1) * 128],
                                    k2sl(ic),
                                    start=w.bank_start(f"pb{dl}") if (p == 0 and kcl == 0) else False,
                                    stop=(p == 7 and kcl == 3)))
                        PSd.end(h * 8 + p)
                    w.wait(V, "sT", w.c["sT"])
                    for dl in range(4):
                        f = vopd if dl == 0 else vop
                        f(lambda v, dl=dl: v.tensor_tensor(
                            QSCW[:, dl * TC:(dl + 1) * TC], pb[dl][:, 0:TC], BRDG[:], ALU.mult))
                    for dl in range(4):
                        dout = h * 4 + dl
                        f = vopd if dl == 0 else vop
                        f(lambda v, dout=dout, dl=dl: v.tensor_tensor(
                            R[:, dout * TC:(dout + 1) * TC], R[:, dout * TC:(dout + 1) * TC],
                            QSCW[:, dl * TC:(dl + 1) * TC], ALU.add))
                    for dl in range(4):
                        w.bank_consumed(f"pb{dl}")
                w.bar()

            def emit_zab(decay_global=None):
                ZS = QSCW
                w.wait(T, "sV", w.c["sV"])
                w.wait(T, "sS", w.c["sS"])
                w.bank_open("pstat")
                for j in range(NT):
                    for c in range(KC):
                        top(lambda t, c=c, j=j: MMSG(t,
                            pstat[:, 256 + j * 32: 256 + (j + 1) * 32],
                            Y2B[:, c * TC + j * 128: c * TC + (j + 1) * 128],
                            WZ[:, c * 32:(c + 1) * 32],
                            start=w.bank_start("pstat") if (j == 0 and c == 0) else False,
                            stop=(c == KC - 1)))
                w.wait(V, "sT", w.c["sT"])
                for j in range(NT):
                    f = vopd if j == 0 else vop
                    f(lambda v, j=j: v.tensor_tensor(
                        ZS[:, j * 32:(j + 1) * 32], pstat[:, 256 + j * 32: 256 + (j + 1) * 32],
                        ZBB[:], ALU.add))
                w.bank_consumed("pstat")
                w.wait(S, "sV", w.c["sV"])
                zview = ZS[:, 0:NT * 32].rearrange("p (t z) -> p t z", z=32)
                gview = GT8[:, 0:2 * NT].rearrange("p (t z) -> p t z", z=2)
                sop(lambda s: s.activation(gview, zview[:, :, 24:26], AF.Exp, scale=CON[:, 1:2]))
                w.wait(V, "sS", w.c["sS"])
                vopd(lambda v: v.tensor_scalar_add(GT8[:, 0:2 * NT], GT8[:, 0:2 * NT], 1.0))
                vopd(lambda v: v.reciprocal(GT8[:, 0:2 * NT], GT8[:, 0:2 * NT]))
                segs = []
                for j in range(NT):
                    segs.append((j * 32, NREG, 2 * j))
                    segs.append((j * 32 + 8, NMEM, 2 * j + 1))
                w.do(V, lambda v: v.drain())
                for (c0, n, k) in segs:
                    vop(lambda v, c0=c0, n=n, k=k: v.tensor_reduce(
                        SMX[:, 8 + k: 9 + k], ZS[:, c0:c0 + n], AX.X, ALU.max, negate=True))
                w.wait(S, "sV", w.c["sV"])
                for (c0, n, k) in segs:
                    sop(lambda s, c0=c0, n=n, k=k: s.activation(
                        ZS[:, c0:c0 + n], ZS[:, c0:c0 + n], AF.Exp, bias=SMX[:, 8 + k: 9 + k]))
                w.wait(V, "sS", w.c["sS"])
                for (c0, n, k) in segs:
                    vop(lambda v, c0=c0, n=n, k=k: v.tensor_reduce(
                        SMX[:, 16 + k: 17 + k], ZS[:, c0:c0 + n], AX.X, ALU.add))
                w.do(V, lambda v: v.drain())
                for (c0, n, k) in segs:
                    vop(lambda v, k=k: v.reciprocal(SMX[:, 16 + k: 17 + k], SMX[:, 16 + k: 17 + k]))
                w.do(V, lambda v: v.drain())
                for (c0, n, k) in segs:
                    vop(lambda v, c0=c0, n=n, k=k: v.tensor_scalar(
                        ZS[:, c0:c0 + n], ZS[:, c0:c0 + n], SMX[:, 16 + k: 17 + k], None, ALU.mult))
                w.do(V, lambda v: v.drain())
                for j in range(NT):
                    vop(lambda v, j=j: v.tensor_scalar(
                        ABT[:, j * 32: j * 32 + 8], ZS[:, j * 32: j * 32 + 8],
                        GT8[:, 2 * j: 2 * j + 1], None, ALU.mult))
                    vop(lambda v, j=j: v.tensor_scalar(
                        ABT[:, j * 32 + 8: j * 32 + 24], ZS[:, j * 32 + 8: j * 32 + 24],
                        GT8[:, 2 * j + 1: 2 * j + 2], None, ALU.mult))
                if decay_global is not None:
                    w.do(V, lambda v: v.drain())
                    for j in range(NT):
                        vop(lambda v, j=j, gci=decay_global[j]: v.tensor_scalar(
                            ABT[:, j * 32 + 8: j * 32 + 24], ABT[:, j * 32 + 8: j * 32 + 24],
                            DVF[:, gci:gci + 1], None, ALU.mult))
                w.wait(S, "sV", w.c["sV"])
                for j in range(NT):
                    sop(lambda s_, j=j: s_.activation(
                        ABF[:, j * 32: j * 32 + 24], ABT[:, j * 32: j * 32 + 24],
                        AF.Identity))

            def precompute_srcs():
                srcs = []
                for mq_ in (wvrq, wqrq, wvmq, wqmq):
                    srcs.append((fmview(mq_, 0, D, 0, 512), 512))
                    srcs.append((fmview(mq_, 0, D, 512, 512), 512))
                return srcs

            def emit_precompute(PS):
                mats = [(wvrq, RVB, BVr), (wqrq, RQB, BQr), (wvmq, MVB, BVm), (wqmq, MQB, BQm)]
                w.wait(T, "sV", w.c["sV"])
                w.wait(T, "sS", w.c["sS"])
                for mi, (mq_, dstb, bias) in enumerate(mats):
                    for hhalf in range(2):
                        buf = PS.begin(mi * 2 + hhalf)
                        for dl in range(4):
                            dout = hhalf * 4 + dl
                            bank = f"pb{dl % 2}"
                            w.bank_open(bank)
                            for c in range(KC):
                                top(lambda t, c=c, dl=dl, buf=buf: MMSG(t, 
                                    pb[dl % 2][:, 0:TC],
                                    buf[:, c * 512 + dl * 128: c * 512 + (dl + 1) * 128],
                                    Y2B[:, c * TC:(c + 1) * TC],
                                    start=w.bank_start(bank) if c == 0 else False,
                                    stop=(c == KC - 1)))
                            w.wait(S, "sT", w.c["sT"])
                            sop(lambda s, dout=dout, dstb=dstb, bias=bias, dl=dl: s.activation(
                                dstb[:, dout * TC:(dout + 1) * TC], pb[dl % 2][:, 0:TC],
                                AF.Identity, bias=bias[:, dout:dout + 1]))
                            w.bank_consumed(bank, "sS")
                        PS.end(mi * 2 + hhalf)
                w.bar()

            def emit_scan_chunk(j, mode="use", g=0):
                a_sl = ABF[:, j * 32: j * 32 + 8]
                b_sl = ABF[:, j * 32 + 8: j * 32 + 24]

                def ch(buf, c):
                    return buf[:, c * TC + j * 128: c * TC + (j + 1) * 128]

                if mode == "agg":
                    w.wait(T, "sV", w.c["sV"])
                    w.wait(T, "sS", w.c["sS"])
                    w.bank_open("ptr")
                    for c in range(KC):
                        top(lambda t, c=c: t.transpose(
                            ptr[:, c * 128:(c + 1) * 128], ch(RVB, c), IDN[:]))
                        if c == 0:
                            w.bank_started["ptr"] = True
                    w.wait(S, "sT", w.c["sT"])
                    for c in range(KC):
                        sop(lambda s_, c=c: s_.activation(
                            VrTM[:, c * 128:(c + 1) * 128],
                            ptr[:, c * 128:(c + 1) * 128], AF.Identity))
                    w.bank_consumed("ptr", "sS")
                    w.wait(T, "sS", w.c["sS"])
                    w.bank_open("ptr")
                    for c in range(KC):
                        top(lambda t, c=c: t.transpose(
                            ptr[:, c * 128:(c + 1) * 128], ch(MVB, c), IDN[:]))
                    w.wait(S, "sT", w.c["sT"])
                    for c in range(KC):
                        sop(lambda s_, c=c: s_.activation(
                            VmTM[:, c * 128:(c + 1) * 128],
                            ptr[:, c * 128:(c + 1) * 128], AF.Identity))
                    w.bank_consumed("ptr", "sS")
                    vopd(lambda v: v.tensor_scalar(BDCB[:], ABT[:, j * 32 + 8: j * 32 + 24],
                                                   DDV[:, 0:1], None, ALU.mult))
                    w.wait(T, "sV", w.c["sV"])
                    w.wait(T, "sS", w.c["sS"])
                    w.bank_open("pstat")
                    for c in range(KC):
                        top(lambda t, c=c: MMSG(t,
                            pstat[:, 64 + c * 8: 64 + (c + 1) * 8],
                            VrTM[:, c * 128:(c + 1) * 128], a_sl,
                            start=w.bank_start("pstat") if c == 0 else False, stop=True))
                    for c in range(KC):
                        top(lambda t, c=c: MMSG(t,
                            pstat[:, 192 + c * 16: 192 + (c + 1) * 16],
                            VmTM[:, c * 128:(c + 1) * 128], BDCB[:],
                            start=False, stop=True))
                    w.wait(V, "sT", w.c["sT"])
                    w.wait(V, "sS", w.c["sS"])
                    vopd(lambda v: v.tensor_tensor(SRF[:], SRF[:], pstat[:, 64:128], ALU.add))
                    vop(lambda v: v.tensor_scalar(SMF[:], SMF[:], DEC128, None, ALU.mult))
                    vopd(lambda v: v.tensor_tensor(SMF[:], SMF[:], pstat[:, 192:320], ALU.add))
                    w.bank_consumed("pstat")
                    w.wait(S, "sV", w.c["sV"])
                    sop(lambda s_, j=j: s_.activation(
                        SLOC[:, j * 192: j * 192 + 64], SRF[:], AF.Identity))
                    sop(lambda s_, j=j: s_.activation(
                        SLOC[:, j * 192 + 64: (j + 1) * 192], SMF[:], AF.Identity))
                    return

                w.wait(V, "sS", w.c["sS"])
                if g == 0:
                    vopd(lambda v: v.tensor_copy(SRF[:], SINI[:, 0:64]))
                    vop(lambda v: v.tensor_copy(SMF[:], SINI[:, 64:192]))
                else:
                    dpow = float(DECAY ** (128 * g))
                    vopd(lambda v, j=j: v.tensor_tensor(
                        SRF[:], SINI[:, 0:64], SLOC[:, j * 192: j * 192 + 64], ALU.add))
                    vop(lambda v, j=j, dpow=dpow: v.scalar_tensor_tensor(
                        SMF[:], SINI[:, 64:192], dpow,
                        SLOC[:, j * 192 + 64: (j + 1) * 192], ALU.mult, ALU.add))
                w.wait(S, "sV", w.c["sV"])
                sop(lambda s_: s_.activation(SRFB[:], SRF[:], AF.Identity))
                sop(lambda s_: s_.activation(SMFB[:], SMF[:], AF.Identity))

                w.wait(T, "sV", w.c["sV"])
                w.wait(T, "sS", w.c["sS"])
                w.bank_open("ptr")
                for c in range(KC):
                    top(lambda t, c=c: t.transpose(
                        ptr[0:8, c * 128:(c + 1) * 128], SRFB[:, c * 8:(c + 1) * 8], IDN[:]))
                    if c == 0:
                        w.bank_started["ptr"] = True
                w.wait(S, "sT", w.c["sT"])
                for c in range(KC):
                    sop(lambda s_, c=c: s_.activation(
                        SRNB[0:8, c * 128:(c + 1) * 128], ptr[0:8, c * 128:(c + 1) * 128],
                        AF.Identity))
                w.bank_consumed("ptr", "sS")
                w.wait(T, "sS", w.c["sS"])
                w.bank_open("ptr")
                for c in range(KC):
                    top(lambda t, c=c: t.transpose(
                        ptr[0:16, c * 128:(c + 1) * 128], SMFB[:, c * 16:(c + 1) * 16], IDN[:]))
                w.wait(S, "sT", w.c["sT"])
                for c in range(KC):
                    sop(lambda s_, c=c: s_.activation(
                        SMNB[0:16, c * 128:(c + 1) * 128], ptr[0:16, c * 128:(c + 1) * 128],
                        AF.Identity))
                w.bank_consumed("ptr", "sS")
                w.bank_open("pb5")
                for c in range(KC):
                    top(lambda t, c=c: MMSG(t, 
                        pb[5][:, 0:128], ch(RVB, c), ch(RQB, c),
                        start=w.bank_start("pb5") if c == 0 else False, stop=(c == KC - 1)))
                for c in range(KC):
                    top(lambda t, c=c: MMSG(t, 
                        pb[5][:, 128:256], ch(MVB, c), ch(MQB, c),
                        start=False, stop=(c == KC - 1)))
                w.wait(V, "sT", w.c["sT"])
                vop(lambda v: v.tensor_tensor(P1B[:, 0:128], pb[5][:, 0:128], MKUL[:], ALU.mult))
                vop(lambda v: v.tensor_tensor(P1B[:, 128:256], pb[5][:, 128:256], MKDC[:], ALU.mult))
                w.wait(T, "sV", w.c["sV"])
                w.wait(T, "sS", w.c["sS"])
                w.bank_open("pstat")
                for c in range(KC):
                    top(lambda t, c=c: MMSG(t, 
                        pstat[:, 0:8], ch(RQB, c), SRFB[:, c * 8:(c + 1) * 8],
                        start=w.bank_start("pstat") if c == 0 else False, stop=False))
                top(lambda t: MMSG(t, pstat[:, 0:8], P1B[:, 0:128], a_sl,
                                       start=False, stop=True))
                for c in range(KC):
                    top(lambda t, c=c: MMSG(t, 
                        pstat[:, 16:32], ch(MQB, c), SMFB[:, c * 16:(c + 1) * 16],
                        start=False, stop=(c == KC - 1)))
                top(lambda t: MMSG(t, pstat[:, 32:48], P1B[:, 128:256], b_sl,
                                       start=False, stop=True))
                w.wait(V, "sT", w.c["sT"])
                vopd(lambda v: v.tensor_reduce(SMX[:, 0:1], pstat[:, 0:8], AX.X, ALU.max, negate=True))
                vopd(lambda v: v.tensor_scalar(SMX[:, 1:2], SMX[:, 0:1], 0.03125, None, ALU.mult))
                w.wait(S, "sV", w.c["sV"])
                sop(lambda s: s.activation(PR[:], pstat[:, 0:8], AF.Exp,
                                           bias=SMX[:, 1:2], scale=CON[:, 0:1]))
                w.wait(V, "sS", w.c["sS"])
                vop(lambda v: v.tensor_reduce(SMX[:, 2:3], PR[:], AX.X, ALU.add))
                vopd(lambda v: v.reciprocal(SMX[:, 2:3], SMX[:, 2:3]))
                vopd(lambda v: v.tensor_scalar(PR[:], PR[:], SMX[:, 2:3], None, ALU.mult))
                w.wait(S, "sV", w.c["sV"])
                sop(lambda s_: s_.activation(PRB[:], PR[:], AF.Identity))
                vop(lambda v: v.tensor_copy(SCM[:], pstat[:, 32:48]))
                vopd(lambda v: v.scalar_tensor_tensor(SCM[:], pstat[:, 16:32], LAM[:, 0:1],
                                                     SCM[:], ALU.mult, ALU.add))
                w.bank_consumed("pstat")
                vopd(lambda v: v.tensor_reduce(SMX[:, 3:4], SCM[:], AX.X, ALU.max, negate=True))
                vopd(lambda v: v.tensor_scalar(SMX[:, 4:5], SMX[:, 3:4], 0.03125, None, ALU.mult))
                w.wait(S, "sV", w.c["sV"])
                sop(lambda s: s.activation(PM[:], SCM[:], AF.Exp,
                                           bias=SMX[:, 4:5], scale=CON[:, 0:1]))
                w.wait(V, "sS", w.c["sS"])
                vop(lambda v: v.tensor_reduce(SMX[:, 5:6], PM[:], AX.X, ALU.add))
                vopd(lambda v: v.reciprocal(SMX[:, 5:6], SMX[:, 5:6]))
                vopd(lambda v: v.tensor_scalar(PM[:], PM[:], SMX[:, 5:6], None, ALU.mult))
                vopd(lambda v: v.tensor_scalar(PM[:], PM[:], LAM[:, 0:1], None, ALU.mult))
                w.wait(S, "sV", w.c["sV"])
                sop(lambda s_: s_.activation(PMB[:], PM[:], AF.Identity))
                w.wait(T, "sV", w.c["sV"])
                w.wait(T, "sS", w.c["sS"])
                w.bank_open("ptr")
                top(lambda t: t.transpose(ptr[0:8, 0:128], a_sl, IDN[:]))
                w.bank_started["ptr"] = True
                top(lambda t: t.transpose(ptr[0:16, 128:256], b_sl, IDN[:]))
                top(lambda t: t.transpose(ptr[0:8, 256:384], PRB[:], IDN[:]))
                top(lambda t: t.transpose(ptr[0:16, 384:512], PMB[:], IDN[:]))
                w.wait(S, "sT", w.c["sT"])
                sop(lambda s_: s_.activation(TRS[0:8, 0:128], ptr[0:8, 0:128], AF.Identity))
                sop(lambda s_: s_.activation(TRS[0:16, 128:256], ptr[0:16, 128:256], AF.Identity))
                sop(lambda s_: s_.activation(TRS[0:8, 256:384], ptr[0:8, 256:384], AF.Identity))
                sop(lambda s_: s_.activation(TRS[0:16, 384:512], ptr[0:16, 384:512], AF.Identity))
                w.bank_consumed("ptr", "sS")
                w.wait(T, "sV", w.c["sV"])
                w.wait(T, "sS", w.c["sS"])
                w.bank_open("pb5")
                top(lambda t: MMSG(t, pb[5][:, 256:384], TRS[0:8, 0:128], TRS[0:8, 256:384],
                                       start=w.bank_start("pb5"), stop=True))
                top(lambda t: MMSG(t, pb[5][:, 384:512], TRS[0:16, 128:256], TRS[0:16, 384:512],
                                       start=False, stop=True))
                w.wait(V, "sT", w.c["sT"])
                vop(lambda v: v.tensor_tensor(EMB[:, 0:128], pb[5][:, 256:384], MKUL[:], ALU.mult))
                vop(lambda v: v.tensor_tensor(EMB[:, 128:256], pb[5][:, 384:512], MKDE[:], ALU.mult))
                w.bank_consumed("pb5")
                w.wait(T, "sS", w.c["sS"])
                w.bank_open("ptr")
                for c in range(KC):
                    top(lambda t, c=c: t.transpose(
                        ptr[:, c * 128:(c + 1) * 128], ch(RVB, c), IDN[:]))
                    if c == 0:
                        w.bank_started["ptr"] = True
                w.wait(S, "sT", w.c["sT"])
                for c in range(KC):
                    sop(lambda s_, c=c: s_.activation(
                        VrTM[:, c * 128:(c + 1) * 128], ptr[:, c * 128:(c + 1) * 128],
                        AF.Identity))
                w.bank_consumed("ptr", "sS")
                w.wait(T, "sS", w.c["sS"])
                w.bank_open("ptr")
                for c in range(KC):
                    top(lambda t, c=c: t.transpose(
                        ptr[:, c * 128:(c + 1) * 128], ch(MVB, c), IDN[:]))
                w.wait(S, "sT", w.c["sT"])
                for c in range(KC):
                    sop(lambda s_, c=c: s_.activation(
                        VmTM[:, c * 128:(c + 1) * 128], ptr[:, c * 128:(c + 1) * 128],
                        AF.Identity))
                w.bank_consumed("ptr", "sS")
                w.wait(T, "sV", w.c["sV"])
                w.wait(T, "sS", w.c["sS"])
                w.bank_open("pb0")
                w.bank_open("pb1")
                for c in range(KC):
                    bank = pb[0] if c < 4 else pb[1]
                    bname = "pb0" if c < 4 else "pb1"
                    col = (c % 4) * 128
                    top(lambda t, c=c, bank=bank, col=col, bname=bname: MMSG(t, 
                        bank[:, col:col + 128], SRNB[0:8, c * 128:(c + 1) * 128],
                        TRS[0:8, 256:384],
                        start=w.bank_start(bname), stop=False))
                    top(lambda t, c=c, bank=bank, col=col: MMSG(t, 
                        bank[:, col:col + 128], VrTM[:, c * 128:(c + 1) * 128],
                        EMB[:, 0:128], start=False, stop=True))
                w.wait(S, "sT", w.c["sT"])
                for c in range(KC):
                    bank = pb[0] if c < 4 else pb[1]
                    col = (c % 4) * 128
                    sop(lambda s_, c=c, bank=bank, col=col: s_.activation(
                        ch(RRD, c), bank[:, col:col + 128], AF.Identity))
                w.bank_consumed("pb0", "sS")
                w.bank_consumed("pb1", "sS")
                w.bank_open("pb2")
                w.bank_open("pb3")
                for c in range(KC):
                    bank = pb[2] if c < 4 else pb[3]
                    bname = "pb2" if c < 4 else "pb3"
                    col = (c % 4) * 128
                    top(lambda t, c=c, bank=bank, col=col, bname=bname: MMSG(t, 
                        bank[:, col:col + 128], SMNB[0:16, c * 128:(c + 1) * 128],
                        TRS[0:16, 384:512],
                        start=w.bank_start(bname), stop=False))
                    top(lambda t, c=c, bank=bank, col=col: MMSG(t, 
                        bank[:, col:col + 128], VmTM[:, c * 128:(c + 1) * 128],
                        EMB[:, 128:256], start=False, stop=True))
                w.wait(S, "sT", w.c["sT"])
                for c in range(KC):
                    bank = pb[2] if c < 4 else pb[3]
                    col = (c % 4) * 128
                    sop(lambda s_, c=c, bank=bank, col=col: s_.activation(
                        ch(MRD, c), bank[:, col:col + 128], AF.Identity))
                w.bank_consumed("pb2", "sS")
                w.bank_consumed("pb3", "sS")
            def epilogue_srcs():
                return [(fmview(combq, m0 * D, D, h2 * 512, 512), 512)
                        for h2 in range(2) for m0 in range(3)]

            def emit_epilogue(PS):
                rhs_of = [Y2B, RRD, MRD]
                w.wait(T, "sV", w.c["sV"])
                w.wait(T, "sS", w.c["sS"])
                for h2 in range(2):
                    for dl in range(4):
                        w.bank_open(f"pb{dl}")
                    for m0 in range(3):
                        buf = PS.begin(h2 * 3 + m0)
                        for dl in range(4):
                            bname = f"pb{dl}"
                            for c in range(KC):
                                top(lambda t, c=c, dl=dl, buf=buf, m0=m0: MMSG(t,
                                    pb[dl][:, 0:TC],
                                    buf[:, c * 512 + dl * 128: c * 512 + (dl + 1) * 128],
                                    rhs_of[m0][:, c * TC:(c + 1) * TC],
                                    start=w.bank_start(bname) if (m0 == 0 and c == 0) else False,
                                    stop=(m0 == 2 and c == KC - 1)))
                        PS.end(h2 * 3 + m0)
                    w.wait(S, "sT", w.c["sT"])
                    for dl in range(4):
                        dout = h2 * 4 + dl
                        sop(lambda s_, dout=dout, dl=dl: s_.activation(
                            ARN[:, dout * TC:(dout + 1) * TC], pb[dl][:, 0:TC],
                            AF.Identity, bias=CBB[:, dout:dout + 1]))
                    w.wait(V, "sS", w.c["sS"])
                    for dl in range(4):
                        dout = h2 * 4 + dl
                        f = vopd if dl == 0 else vop
                        f(lambda v, dout=dout: v.tensor_tensor(
                            ARN[:, dout * TC:(dout + 1) * TC], ARN[:, dout * TC:(dout + 1) * TC],
                            Y2F[:, dout * TC:(dout + 1) * TC], ALU.add))
                    for dl in range(4):
                        w.bank_consumed(f"pb{dl}", "sS")
                emit_ln(ARN, ARN, affine=True, add_to=R)

            sCC = "sT"
            for s in range(nsc):
                w.wait(Y, "sV", w.c["sV"])
                wait_all_st(Y)
                lv = load(sb3(R[:], TC), fmview(xT_own, 0, D, s * TC, TC))
                w.wait_t(V, lv)
                w.wait_t(S, lv)
                w.wait_t(T, lv)
                w.bar()
                PSg = emit_lnq(R)
                emit_ffn(PSg)
                if deferred_convs:
                    deferred_convs.pop(0)()
                if stage == "ffn1":
                    stv = store(fmview(yT, 0, D, s * TC, TC), sb3(R[:], TC))
                    continue
                store(fmview(sx1, 0, D, s * TC, TC), sb3(R[:], TC))
                emit_ln(R, Y2F)
                w.wait(S, "sV", w.c["sV"])
                for c in range(KC):
                    sop(lambda s_, c=c: s_.activation(
                        Y2B[:, c * TC:(c + 1) * TC], Y2F[:, c * TC:(c + 1) * TC],
                        AF.Identity))
                store(fmview(sy2, 0, D, s * TC, TC), sb3(Y2F[:, 0:KC * TC], TC))
                store(fmview(sy2b, 0, D, s * TC, TC), sb3(Y2B[:], TC),
                      after=("sS", w.c["sS"]))
                if s == 0:
                    wait_all_st(Y)
                PSp = PieceStream(precompute_srcs())
                emit_zab()
                store(sabf[0:128, s * NT * 32:(s + 1) * NT * 32], ABF[:],
                      after=("sS", w.c["sS"]))
                emit_precompute(PSp)
                for (buf, scr) in ((RVB, srv), (RQB, srq), (MVB, smv), (MQB, smq)):
                    store(fmview(scr, 0, D, s * TC, TC), sb3(buf[:], TC),
                          after=("sS", w.c["sS"]))
                for j in range(NT):
                    emit_scan_chunk(j, mode="agg")
                stv = store(sloc[0:128, s * NT * 192:(s + 1) * NT * 192], SLOC[:],
                            after=("sS", w.c["sS"]))
                w.bar()
            if stage == "ffn1":
                wait_all_st(Gq)
                wait_all_st(Y)
                return

            wait_all_st(V)
            w.wait(V, "sS", w.c["sS"])
            vopd(lambda v: v.tensor_copy(SLOC[:, 0:64], SRF[:]))
            vop(lambda v: v.tensor_copy(SLOC[:, 64:192], SMF[:]))
            ccst = store(ccin.rearrange("o (p n) -> (o p) n", p=128), SLOC[:, 0:192])
            w.wait_t(Gq, ccst)
            i = w.do(Gq, lambda g_: g_.collective_compute(
                "AllGather", mybir.AluOpType.bypass,
                replica_groups=[list(range(n_replicas))],
                ins=[ccin], outs=[ccout]))
            ccv = w.inc("sCCs", i, 1)
            w.wait(Y, "sCCs", ccv)
            lvg = load(GATH[:, 0:n_replicas * 192].rearrange("p (r n) -> p r n", n=192),
                       ccout.rearrange("r (p n) -> p r n", p=128))
            w.wait_t(V, lvg)
            lvs = load(SEL[:], sel_in)
            w.wait_t(V, lvs)
            for r in range(n_replicas):
                if r == 0:
                    vopd(lambda v: v.tensor_scalar(
                        SINI[:], GATH[:, 0:192], SEL[:, 0:1], None, ALU.mult))
                else:
                    vopd(lambda v, r=r: v.scalar_tensor_tensor(
                        SQ2[:, 0:192], GATH[:, r * 192:(r + 1) * 192], SEL[:, r:r + 1],
                        SINI[:], ALU.mult, ALU.add))
                    vopd(lambda v: v.tensor_copy(SINI[:], SQ2[:, 0:192]))
            w.bar()

            for s in range(nsc):
                w.wait(S, "sV", w.c["sV"])
                w.wait(S, "sS", w.c["sS"])
                wait_all_st(S)
                loads = []
                loads.append(loadS(sb3(R[:], TC), fmview(sx1, 0, D, s * TC, TC)))
                loads.append(loadS(sb3(Y2F[:, 0:KC * TC], TC), fmview(sy2, 0, D, s * TC, TC)))
                loads.append(loadS(sb3(Y2B[:], TC), fmview(sy2b, 0, D, s * TC, TC)))
                loads.append(loadS(ABF[:], sabf[0:128, s * NT * 32:(s + 1) * NT * 32]))
                for (buf, scr) in ((RVB, srv), (RQB, srq), (MVB, smv), (MQB, smq)):
                    loads.append(loadS(sb3(buf[:], TC), fmview(scr, 0, D, s * TC, TC)))
                if s == 0:
                    if NT > 1:
                        loads.append(loadS(SLOC[:, 192:NT * 192],
                                           sloc[0:128, 0:(NT - 1) * 192]))
                else:
                    loads.append(loadS(SLOC[:],
                                       sloc[0:128, (s * NT - 1) * 192:(s * NT + NT - 1) * 192]))
                for lv2 in loads:
                    w.wait_t(V, lv2)
                    w.wait_t(T, lv2)
                    w.wait_t(S, lv2)
                w.bar()
                PSe = None
                for j in range(NT):
                    if j == NT - 1:
                        PSe = PieceStream(epilogue_srcs())
                    emit_scan_chunk(j, mode="use", g=s * NT + j)
                emit_epilogue(PSe)
                if stage == "mix":
                    stv = store(fmview(yT, 0, D, s * TC, TC), sb3(R[:], TC))
                    continue
                PSg = emit_lnq(R)
                emit_ffn(PSg)
                stv = store(fmview(yT, 0, D, s * TC, TC), sb3(R[:], TC))
            wait_all_st(Gq)
            wait_all_st(Y)

        block.sync(lambda eng: schedule(Wk("y", eng, sems)))
        block.tensor(lambda eng: schedule(Wk("t", eng, sems)))
        block.vector(lambda eng: schedule(Wk("v", eng, sems)))
        block.scalar(lambda eng: schedule(Wk("s", eng, sems)))
        block.gpsimd(lambda eng: schedule(Wk("g", eng, sems)))

    return nc



def _consts():
    s = np.arange(128)
    t = np.arange(128)
    m_ul = (s[:, None] <= t[None, :]).astype(np.float32)
    m_dc = (DECAY ** (t[None, :] - s[:, None])).astype(np.float32) * m_ul
    m_de = (DECAY ** (-(s[:, None] + 1.0))).astype(np.float32) * m_ul
    lam = (DECAY ** (np.arange(128) + 1.0)).astype(np.float32).reshape(128, 1)
    ddv = (DECAY ** (127.0 - np.arange(128))).astype(np.float32).reshape(128, 1)
    sg = np.arange(HALF).reshape(32, 128).T
    dvf = (DECAY ** (4095.0 - sg)).astype(np.float32)
    return m_ul, m_dc, m_de, lam, ddv, dvf


def _pp(b):
    return np.ascontiguousarray(b.reshape(KC, 128).T.astype(np.float32))


def kernel(**inputs):
    x = np.asarray(inputs["L0_attn_u"], np.float32)
    gw = {k: np.asarray(v, np.float32) for k, v in inputs.items()}
    m_ul, m_dc, m_de, lam, ddv, dvf = _consts()

    wz = np.zeros((32, D), np.float32)
    wz[0:8] = gw["L0_attn_reg_addr_w"]
    wz[8:24] = gw["L0_attn_mem_addr_w"]
    wz[24] = gw["L0_attn_reg_gate_w"][0]
    wz[25] = gw["L0_attn_mem_gate_w"][0]
    zb = np.zeros((1, 32), np.float32)
    zb[0, 0:8] = gw["L0_attn_reg_addr_b"]
    zb[0, 8:24] = gw["L0_attn_mem_addr_b"]
    zb[0, 24] = gw["L0_attn_reg_gate_b"][0]
    zb[0, 25] = gw["L0_attn_mem_gate_b"][0]

    comb = gw["L0_attn_comb_w"]
    shared = dict(
        wgT=np.ascontiguousarray(gw["L0_ffn_w_gate"].T),
        wuT=np.ascontiguousarray(gw["L0_ffn_w_up"].T),
        wdT=np.ascontiguousarray(gw["L0_ffn_w_down"].T),
        wvrT=np.ascontiguousarray(gw["L0_attn_reg_val_w"].T),
        wqrT=np.ascontiguousarray(gw["L0_attn_reg_q_w"].T),
        wvmT=np.ascontiguousarray(gw["L0_attn_mem_val_w"].T),
        wqmT=np.ascontiguousarray(gw["L0_attn_mem_q_w"].T),
        combT=np.ascontiguousarray(comb.T),
        wzT=np.ascontiguousarray(wz.T),
        zbrow=zb,
        bvr_pp=_pp(gw["L0_attn_reg_val_b"]),
        bqr_pp=_pp(gw["L0_attn_reg_q_b"]),
        bvm_pp=_pp(gw["L0_attn_mem_val_b"]),
        bqm_pp=_pp(gw["L0_attn_mem_q_b"]),
        combb_pp=_pp(gw["L0_attn_comb_b"]),
        lng_pp=_pp(gw["L0_attn_ln_g"]),
        lnb_pp=_pp(gw["L0_attn_ln_b"]),
        bvr_row=gw["L0_attn_reg_val_b"].reshape(1, D),
        bvm_row=gw["L0_attn_mem_val_b"].reshape(1, D),
        maskUL=m_ul, maskDec=m_dc, maskDecE=m_de,
        lam_in=lam, ddv_in=ddv,
        ident_in=np.eye(128).astype(ml_dtypes.bfloat16),
        identf_in=np.eye(128, dtype=np.float32),
        ones_in=np.ones((128, 1), np.float32),
        onesbf_in=np.ones((128, 1), ml_dtypes.bfloat16),
        bc1_in=np.ones((1, 128), np.float32),
    )

    nc = build_program()
    in_maps = []
    for c in range(N_CORES):
        beta, h = c // 2, c % 2
        xo = np.ascontiguousarray(x[beta, h * HALF:(h + 1) * HALF, :].T)
        sel = np.zeros((128, 8), np.float32)
        if h == 1:
            sel[:, c - 1] = 1.0
        m = dict(shared)
        m["xT_own"] = xo
        m["sel_in"] = sel
        in_maps.append(m)

    res = run_bass_kernel_spmd(nc, in_maps, core_ids=list(range(N_CORES)))
    out = np.empty((B, L, D), np.float32)
    for c in range(N_CORES):
        beta, h = c // 2, c % 2
        out[beta, h * HALF:(h + 1) * HALF, :] = res.results[c]["yT"].T
    return out
